# version 07
# baseline (speedup 1.0000x reference)
import numpy as np
import ml_dtypes
from contextlib import ExitStack

import concourse.bass as bass
import concourse.mybir as mybir
from concourse.bass_utils import run_bass_kernel_spmd

F32 = mybir.dt.float32
BF16 = mybir.dt.bfloat16
AF = mybir.ActivationFunctionType
ALU = mybir.AluOpType
AX = mybir.AxisListType
BF16NP = ml_dtypes.bfloat16

ENGINES = ("sync", "tensor", "vector", "scalar", "gpsimd")
SEM_OF = {"tensor": "T", "vector": "V", "scalar": "S", "gpsimd": "G"}
ALL_SEMS = ("T", "V", "S", "G", "C", "dS", "dR0", "dR1", "dK", "dM", "dC",
            "dV0", "dV1", "dG", "dW0", "dW1", "dW2", "dW3",
            "dX", "dF", "dSC", "dGA", "dI")
DMA_SEMS = ("dS", "dR0", "dR1", "dK", "dM", "dC", "dV0", "dV1", "dG",
            "dW0", "dW1", "dW2", "dW3", "dX", "dF", "dSC", "dGA", "dI")

D = 2048
DC = D // 128
TLOC = 1024
NT = TLOC // 128
INTER = 4096
NE, KTOP, IEXP, JSH = 8, 2, 1408, 5632
CAP = 2560.0
CLOC = 512
HEADS = 16
SEQ = 4096


class Prog:
    def __init__(self, nc, ctx):
        self.nc = nc
        self.ctx = ctx
        self.items = {e: [] for e in ENGINES}
        self.semv = {s: 0 for s in ALL_SEMS}
        self.sems = {s: ctx.enter_context(nc.semaphore("sem_" + s)) for s in ALL_SEMS}
        self.last = {e: None for e in ENGINES}
        self.dirty = set()

    def op(self, e, build, incs=(), waits=()):
        item = {"build": build, "incs": [], "waits": [w for w in waits if w[1] > 0]}
        post = []
        for s in incs:
            amt = 16 if s in DMA_SEMS else 1
            self.semv[s] += amt
            item["incs"].append((s, amt))
            post.append(self.semv[s])
        self.items[e].append(item)
        self.last[e] = item
        self.dirty.add(e)
        return post[0] if len(post) == 1 else post

    def wait(self, e, s, v):
        if v <= 0:
            return
        self.items[e].append({"build": None, "incs": [], "waits": [(s, v)]})

    def vop(self, build, extra_waits=()):
        w = [("V", self.semv["V"])]
        return self.op("vector", build, incs=("V",), waits=list(extra_waits) + w)

    def sop(self, build, extra_waits=()):
        w = [("S", self.semv["S"])]
        return self.op("scalar", build, incs=("S",), waits=list(extra_waits) + w)

    def gop(self, build, extra_waits=()):
        w = [("G", self.semv["G"])]
        return self.op("gpsimd", build, incs=("G",), waits=list(extra_waits) + w)

    def top(self, build, incs=(), waits=()):
        return self.op("tensor", build, incs=incs, waits=waits)

    def dma(self, queue, out_ap, in_ap, waits=(), sem="dS", slow_ok=False):
        return self.op(queue, lambda e, o=out_ap, i=in_ap, s=slow_ok: e.dma_start(
            out=o, in_=i, allow_slow_non_contiguous=s), incs=(sem,), waits=waits)

    def barrier(self):
        for e in list(self.dirty):
            it = self.last[e]
            if it is None or it["build"] is None:
                continue
            s = SEM_OF.get(e)
            if s is None:
                continue
            if any(x[0] in DMA_SEMS for x in it["incs"]):
                continue
            if any(x[0] == s for x in it["incs"]):
                continue
            self.semv[s] += 1
            it["incs"].append((s, 1))
        self.dirty.clear()
        wl = [(s, self.semv[s]) for s in ALL_SEMS if self.semv[s] > 0]
        for e in ENGINES:
            own = SEM_OF.get(e)
            for (s, v) in wl:
                if s == own:
                    continue
                self.wait(e, s, v)

    def finalize(self):
        block = self.ctx.enter_context(self.nc.Block())
        for e in ENGINES:
            items = self.items[e]

            def make(items=items):
                def fn(eng):
                    for it in items:
                        for (s, v) in it["waits"]:
                            eng.wait_ge(self.sems[s], v)
                        if it["build"] is not None:
                            ins = it["build"](eng)
                            for (s, amt) in it["incs"]:
                                ins.then_inc(self.sems[s], amt)
                return fn
            getattr(block, e)(make())


class WBuf:

    def __init__(self, P, name, nslots, elems, st=None):
        self.P = P
        self.nslots = nslots
        self.elems = elems
        self.t = (st or P.ctx).enter_context(
            P.nc.sbuf_tensor(name, [128, nslots * elems], BF16))
        self.free_at = [("T", 0)] * nslots
        self.ready = [0] * nslots
        self.shape = [None] * nslots
        self.rr = 0

    def load(self, dram_ap, C, W, cast=True):
        P = self.P
        slot = self.rr
        self.rr = (self.rr + 1) % self.nslots
        assert C * W <= self.elems, (C, W, self.elems)
        dst = self.t[:, slot * self.elems: slot * self.elems + C * W]
        dst = dst.rearrange("p (c w) -> p c w", w=W)
        sem, val = self.free_at[slot]
        self.ready[slot] = (f"dW{slot}",
                            P.dma("gpsimd", dst, dram_ap, waits=[(sem, val)],
                                  sem=f"dW{slot}"))
        self.shape[slot] = (C, W)
        return slot

    def view(self, slot):
        C, W = self.shape[slot]
        return self.t[:, slot * self.elems: slot * self.elems + C * W].rearrange(
            "p (c w) -> p c w", w=W)

    def mark_free(self, slot, sem, val):
        self.free_at[slot] = (sem, val)


_N_PHASES = 6


def build_program(n_phases=None):
    n_phases = _N_PHASES if n_phases is None else n_phases
    nc = bass.Bass("TRN2", target_bir_lowering=False, debug=False)
    ctx = ExitStack()
    P = Prog(nc, ctx)

    def dt(name, shape, dty=F32, kind="ExternalInput"):
        return nc.dram_tensor(name, shape, dty, kind=kind).ap()

    x0 = dt("x0", [TLOC, D])
    w_in_t = dt("w_in_t", [D, 2 * INTER])
    conv_t = dt("conv_t", [INTER, INTER])
    w_out_t = dt("w_out_t", [INTER, D])
    dvec = dt("dvec", [128, INTER // 128])
    router = dt("router", [D, NE])
    w_upg = dt("w_upg", [NE, D, 2 * IEXP])
    w_dn = dt("w_dn", [NE, IEXP, D])
    sh_up = dt("sh_up", [D, 2 * JSH])
    sh_dn = dt("sh_dn", [JSH, D])
    wq = dt("wq", [D, D]); wk = dt("wk", [D, D]); wv = dt("wv", [D, D]); wo = dt("wo", [D, D])
    l1w_t = dt("l1w_t", [D, D])
    l1b_row = dt("l1b_row", [1, D])
    mask_in = dt("mask_in", [2, 32, 128, 512], mybir.dt.uint8)
    offa_in = dt("offa_in", [16, 1])
    offb_in = dt("offb_in", [16, 1])
    ident_in = dt("ident_in", [128, 128], BF16)
    onescol_in = dt("onescol_in", [128, 1], BF16)
    onescolf_in = dt("onescolf_in", [128, 1])
    onesrowf_in = dt("onesrowf_in", [1, 128])
    iota8_in = dt("iota8_in", [128, NE])
    iota9_in = dt("iota9_in", [128, NE])
    iotat_in = dt("iotat_in", [128, NT])
    padfill_in = dt("padfill_in", [128, 2 * NE * CLOC // 128])
    big8_in = dt("big8_in", [128, NE])
    neg8_in = dt("neg8_in", [128, NE])
    onestile_in = dt("onestile_in", [128, 512], BF16)
    u128_in = dt("u128_in", [128, 128], BF16)
    epscol_in = dt("epscol_in", [128, 1])

    out_d = dt("out", [TLOC, D], F32, kind="ExternalOutput")

    r_d = [dt(f"rr{i}", [TLOC, D], F32, kind="Internal") for i in range(5)]
    kv_loc = dt("kv_loc", [HEADS, 2 * 128 * TLOC], BF16, kind="Internal")
    kv_gath = dt("kv_gath", [4, HEADS, 2 * 128 * TLOC], BF16, kind="Internal")
    cnt_loc = dt("cnt_loc", [2, NE], F32, kind="Internal")
    cnt_gath = dt("cnt_gath", [16, NE], F32, kind="Internal")
    weff_d = dt("weff_d", [D, D], BF16, kind="Internal")
    xh_d = dt("xh_d", [TLOC, D], BF16, kind="Internal")
    tokws_d = dt("tokws_d", [NE * CLOC, 2], F32, kind="Internal")

    _name_ctr = [0]

    def sb(name, shape, dty=F32, st=ctx):
        if name is None:
            _name_ctr[0] += 1
            name = f"sbt{_name_ctr[0]}"
        return st.enter_context(nc.sbuf_tensor(name, shape, dty))

    def psum(name, shape, dty=F32):
        return ctx.enter_context(nc.psum_tensor(name, shape, dty))

    ident = sb("ident", [128, 128], BF16)
    u128 = sb("u128", [128, 128], BF16)
    onescol = sb("onescol", [128, 1], BF16)
    onescolf = sb("onescolf", [128, 1])
    onesrowf = sb("onesrowf", [1, 128])
    iota8 = sb("iota8", [128, NE])
    iota9 = sb("iota9", [128, NE])
    iotat = sb("iotat", [128, NT])
    padfill = sb("padfill", [128, 2 * NE * CLOC // 128])
    big8 = sb("big8", [128, NE])
    neg8 = sb("neg8", [128, NE])
    onestile = sb("onestile", [128, 512], BF16)
    offa = sb("offa", [16, 1])
    offb = sb("offb", [16, 1])
    dD = sb("dD", [128, INTER // 128])
    l1b = sb("l1b", [1, D])
    xh_fm = sb("xh_fm", [128, DC, TLOC], BF16)
    epscol = sb("epscol", [128, 1])

    psfree = {}
    _ps_refs = []
    _ps_ctr = [0]

    def ps_new(st, shape=None, dty=F32):
        _ps_ctr[0] += 1
        t = st.enter_context(nc.psum_tensor(f"ps{_ps_ctr[0]}",
                                            shape or [128, 512], dty))
        psfree[id(t)] = ("T", 0)
        _ps_refs.append(t)
        return t

    def mmg(ps_full, out_ap, pairs, waits=(), inc=True, track=True):
        w = list(waits)
        if track:
            w.append(psfree[id(ps_full)])
        n = len(pairs)
        tval = None
        for i, (lhsT, rhs) in enumerate(pairs):
            ww = w if i == 0 else []
            incs = ("T",) if (inc and i == n - 1) else ()
            r = P.top(lambda e, l=lhsT, rr=rhs, st=(i == 0), sp=(i == n - 1), o=out_ap:
                      e.matmul(o, l, rr, start=st, stop=sp), incs=incs, waits=ww)
            if incs:
                tval = r
        return tval

    def consumed(ps_full, sem, val):
        psfree[id(ps_full)] = (sem, val)

    for dst, src in [(ident, ident_in), (u128, u128_in), (onescol, onescol_in),
                     (onescolf, onescolf_in), (onesrowf, onesrowf_in),
                     (iota8, iota8_in), (iota9, iota9_in),
                     (iotat, iotat_in), (padfill, padfill_in),
                     (big8, big8_in), (neg8, neg8_in),
                     (onestile, onestile_in), (offa, offa_in), (offb, offb_in),
                     (dD, dvec), (l1b, l1b_row), (epscol, epscol_in)]:
        P.dma("sync", dst[:], src)
    P.barrier()

    def emit_ln2(r_tile_fn, xh_tok, lctx, ln_apply_vs=None):
        psT0 = ps_new(lctx, [128, 1024], BF16)
        psT1 = ps_new(lctx, [128, 1024], BF16)
        stats = sb(None, [128, 4 * 6], st=lctx)
        aggr = sb(None, [128, 2], st=lctx)
        sd = sb(None, [128, 1], st=lctx)
        rs = sb(None, [128, 1], st=lctx)
        prev_sqrt_s = 0
        prev_recip_v = 0
        for tt in range(NT):
            rt, rwait = r_tile_fn(tt)
            for i in range(4):
                P.vop(lambda e, i=i, rt=rt: e.bn_stats(
                    out=stats[:, i * 6:(i + 1) * 6], in_=rt[:, i * 512:(i + 1) * 512]),
                    extra_waits=([rwait] if (i == 0 and rwait) else []))
            P.vop(lambda e: e.bn_aggr(out=aggr[:], in_=stats[:]),
                  extra_waits=[("S", prev_sqrt_s)])
            vv = P.semv["V"]
            prev_sqrt_s = P.sop(lambda e: e.activation(out=sd[:], in_=aggr[:, 1:2],
                                                       func=AF.Sqrt, bias=epscol[:, 0:1]),
                                extra_waits=[("V", vv), ("V", prev_recip_v)])
            prev_recip_v = P.vop(lambda e: e.reciprocal(out=rs[:], in_=sd[:]),
                                 extra_waits=[("S", prev_sqrt_s)])
            vA = P.vop(lambda e, tt=tt, rt=rt: e.tensor_scalar(
                out=xh_tok[:, tt, :], in0=rt, scalar1=aggr[:, 0:1], scalar2=rs[:, 0:1],
                op0=ALU.subtract, op1=ALU.mult))
            if ln_apply_vs is not None:
                ln_apply_vs[tt] = vA
            for c in range(DC):
                pst = psT0 if c % 2 == 0 else psT1
                tv = P.top(lambda e, tt=tt, c=c, pst=pst: e.transpose(
                    pst[:, 0:128], xh_tok[:, tt, c * 128:(c + 1) * 128], ident[:]),
                    incs=("T",),
                    waits=([("V", vA)] if c == 0 else []) + [psfree[id(pst)]])
                sv = P.sop(lambda e, tt=tt, c=c, pst=pst: e.copy(
                    out=xh_fm[:, c, tt * 128:(tt + 1) * 128], in_=pst[:, 0:128]),
                    extra_waits=[("T", tv)])
                consumed(pst, "S", sv)

    def phase_moe(r_src_d, r_dst_d, call_idx):
        with ExitStack() as pctx:
            r_res = sb(None, [128, NT, D], st=pctx)
            rsrc = r_src_d.rearrange("(t p) d -> p t d", p=128)
            rdst = r_dst_d.rearrange("(t p) d -> p t d", p=128)
            rdst_rows = r_dst_d
            ldv = P.dma("sync", r_res[:, :, :], rsrc, sem="dR0")
            xdv = 0
            with ExitStack() as lctx:
                xh_tok = sb(None, [128, NT, D], BF16, st=lctx)
                lavs = [0] * NT
                emit_ln2(lambda tt: (r_res[:, tt, :], ("dR0", ldv)), xh_tok, lctx,
                         ln_apply_vs=lavs)
                xh_dv = xh_d.rearrange("(t p) d -> p t d", p=128)
                for tt in range(NT):
                    xdv = P.dma("sync", xh_dv[:, tt, :], xh_tok[:, tt, :],
                                waits=[("V", lavs[tt])], sem="dX")
                P.barrier()

            wb = WBuf(P, f"wb_moe{call_idx}", 4, 12 * 512, st=pctx)
            rout_ctx = ExitStack()
            pb = [ps_new(pctx) for _ in range(6)]
            pb += [ps_new(rout_ctx) for _ in range(2)]
            rot = {"g": 0, "u": 0, "d": 0}
            h_ar = sb(None, [128, 11, TLOC], BF16, st=pctx)
            rt_sb = sb(None, [128, DC, NE], BF16, st=pctx)
            lg = sb(None, [128, NT, NE], st=pctx)
            c1a = sb(None, [128, NT, NE], st=pctx)
            c2a = sb(None, [128, NT, NE], st=pctx)
            sbf = sb(None, [128, NE], BF16, st=pctx)
            xloc = sb(None, [128, NT, NE], st=pctx)
            xg = sb(None, [128, NE], st=pctx)
            gsc = sb(None, [128, NT, NE], st=pctx)
            run = sb(None, [1, NT + 1, NE], st=pctx)
            snap = sb(None, [1, 2, NE], st=pctx)
            cg_sb = sb(None, [16, NE], st=pctx)
            offr = sb(None, [1, 2, NE], st=pctx)
            obase = sb(None, [1, NE], st=pctx)
            dd = sb(None, [128, NT], st=pctx)
            m1 = sb(None, [128, 1], st=pctx); m2 = sb(None, [128, 1], st=pctx)
            eq = sb(None, [128, NE], st=pctx); cand = sb(None, [128, NE], st=pctx)
            idx = sb(None, [128, 1], st=pctx); lg2 = sb(None, [128, NE], st=pctx)
            cw1 = sb(None, [128, NT], st=pctx); cw2 = sb(None, [128, NT], st=pctx)
            pos = sb(None, [128, 1], st=pctx); vld = sb(None, [128, 1], st=pctx)
            wk1 = sb(None, [128, 1], st=pctx); wk2 = sb(None, [128, 1], st=pctx)
            t8 = sb(None, [128, NE], st=pctx)
            gq = sb(None, [128, 2, 512], BF16, st=pctx)

            rv = P.dma("gpsimd", rt_sb[:, :, :],
                       router.rearrange("(c p) e -> p c e", p=128), sem="dG")
            P.vop(lambda e: e.memset(run[0:1, 0, :], 0.0))
            for tt in range(NT):
                pairs = [(xh_fm[:, c, tt * 128:(tt + 1) * 128], rt_sb[:, c, :])
                         for c in range(DC)]
                tv = mmg(pb[5], pb[5][:, 0:NE], pairs,
                         waits=[("dG", rv)] if tt == 0 else ())
                vv = P.vop(lambda e, tt=tt: e.tensor_copy(out=lg[:, tt, :],
                                                          in_=pb[5][:, 0:NE]),
                           extra_waits=[("T", tv)])
                consumed(pb[5], "V", vv)
                P.vop(lambda e, tt=tt: e.tensor_reduce(out=m1[:], in_=lg[:, tt, :],
                                                       axis=AX.X, op=ALU.max))
                P.vop(lambda e, tt=tt: e.tensor_scalar(out=eq[:], in0=lg[:, tt, :],
                                                       scalar1=m1[:, 0:1], scalar2=None,
                                                       op0=ALU.is_equal))
                P.vop(lambda e: e.scalar_tensor_tensor(out=cand[:], in0=eq[:],
                                                       scalar=-9.0, in1=iota9[:],
                                                       op0=ALU.mult, op1=ALU.add))
                P.vop(lambda e: e.tensor_reduce(out=idx[:], in_=cand[:], axis=AX.X,
                                                op=ALU.min))
                P.vop(lambda e, tt=tt: e.tensor_scalar(out=c1a[:, tt, :], in0=iota8[:],
                                                       scalar1=idx[:, 0:1], scalar2=None,
                                                       op0=ALU.is_equal))
                P.vop(lambda e, tt=tt: e.scalar_tensor_tensor(
                    out=lg2[:], in0=c1a[:, tt, :], scalar=-1e30, in1=lg[:, tt, :],
                    op0=ALU.mult, op1=ALU.add))
                P.vop(lambda e: e.tensor_reduce(out=m2[:], in_=lg2[:], axis=AX.X,
                                                op=ALU.max))
                P.vop(lambda e: e.tensor_scalar(out=eq[:], in0=lg2[:], scalar1=m2[:, 0:1],
                                                scalar2=None, op0=ALU.is_equal))
                P.vop(lambda e: e.scalar_tensor_tensor(out=cand[:], in0=eq[:],
                                                       scalar=-9.0, in1=iota9[:],
                                                       op0=ALU.mult, op1=ALU.add))
                P.vop(lambda e: e.tensor_reduce(out=idx[:], in_=cand[:], axis=AX.X,
                                                op=ALU.min))
                P.vop(lambda e, tt=tt: e.tensor_scalar(out=c2a[:, tt, :], in0=iota8[:],
                                                       scalar1=idx[:, 0:1], scalar2=None,
                                                       op0=ALU.is_equal))
                P.vop(lambda e, tt=tt: e.tensor_tensor(out=dd[:, tt:tt + 1], in0=m1[:],
                                                       in1=m2[:], op=ALU.subtract))
                P.vop(lambda e, tt=tt: e.tensor_tensor(out=eq[:], in0=c1a[:, tt, :],
                                                       in1=c2a[:, tt, :], op=ALU.add))
                vS = P.vop(lambda e: e.tensor_copy(out=sbf[:], in_=eq[:]))
                tv2 = mmg(pb[6], pb[6][:, 0:NE], [(u128[:], sbf[:])], waits=[("V", vS)])
                tv3 = mmg(pb[7], pb[7][0:1, 0:NE], [(onescol[:], sbf[:])], waits=[])
                vv = P.vop(lambda e, tt=tt: e.tensor_copy(out=xloc[:, tt, :],
                                                          in_=pb[6][:, 0:NE]),
                           extra_waits=[("T", tv2)])
                consumed(pb[6], "V", vv)
                vv = P.vop(lambda e, tt=tt: e.tensor_tensor(out=run[0:1, tt + 1, :],
                                                            in0=run[0:1, tt, :],
                                                            in1=pb[7][0:1, 0:NE],
                                                            op=ALU.add),
                           extra_waits=[("T", tv3)])
                consumed(pb[7], "V", vv)
            P.vop(lambda e: e.tensor_copy(out=snap[0:1, 0, :], in_=run[0:1, 4, :]))
            vSn = P.vop(lambda e: e.tensor_tensor(out=snap[0:1, 1, :],
                                                  in0=run[0:1, NT, :],
                                                  in1=run[0:1, 4, :], op=ALU.subtract))
            dv = P.dma("sync", cnt_loc, snap[0:1, :, :], waits=[("V", vSn)], sem="dC")
            cv = P.op("gpsimd", lambda e: e.collective_compute(
                "AllGather", ALU.bypass, replica_groups=[list(range(8))],
                ins=[cnt_loc.opt()], outs=[cnt_gath.opt()]),
                incs=("C",), waits=[("dC", dv), ("G", P.semv["G"])])
            dv2 = P.dma("sync", cg_sb[:], cnt_gath, waits=[("C", cv)], sem="dC")
            svw = P.sop(lambda e: e.activation(out=cw1[:, 0:NT], in_=dd[:, 0:NT],
                                               func=AF.Sigmoid),
                        extra_waits=[("V", P.semv["V"])])
            P.vop(lambda e: e.tensor_scalar(out=cw2[:, 0:NT], in0=cw1[:, 0:NT],
                                            scalar1=-1.0, scalar2=1.0, op0=ALU.mult,
                                            op1=ALU.add), extra_waits=[("S", svw)])

            def evac_add(tt, fg, ps_full, ps_ap, tval, scalar_ap=None, const=None):
                rslice = r_res[:, tt, fg * 512:(fg + 1) * 512]
                s = scalar_ap if scalar_ap is not None else const
                vv = P.vop(lambda e, r=rslice, p=ps_ap, s=s: e.scalar_tensor_tensor(
                    out=r, in0=p, scalar=s, in1=r, op0=ALU.mult, op1=ALU.add),
                    extra_waits=[("T", tval)])
                consumed(ps_full, "V", vv)
                return vv

            gq_read = {0: 0, 1: 0}

            def up_and_h(gate_src, up_src, nj):
                for jt in range(nj):
                    gslab = wb.load(gate_src(jt), DC, 128)
                    uslab = wb.load(up_src(jt), DC, 128)
                    gv = wb.view(gslab); uv = wb.view(uslab)
                    lastv = 0
                    for th in range(2):
                        pg = pb[rot["g"] % 3]; rot["g"] += 1
                        pu = pb[3 + rot["u"] % 3]; rot["u"] += 1
                        tg = mmg(pg, pg[:, :],
                                 [(gv[:, c, :], xh_fm[:, c, th * 512:(th + 1) * 512])
                                  for c in range(DC)],
                                 waits=[wb.ready[gslab]])
                        tu = mmg(pu, pu[:, :],
                                 [(uv[:, c, :], xh_fm[:, c, th * 512:(th + 1) * 512])
                                  for c in range(DC)],
                                 waits=[wb.ready[uslab]])
                        sv = P.sop(lambda e, pg=pg, th=th: e.activation(
                            out=gq[:, th, :], in_=pg[:, :], func=AF.Silu),
                            extra_waits=[("T", tg), ("V", gq_read[th])])
                        consumed(pg, "S", sv)
                        vv = P.vop(lambda e, pu=pu, th=th, jt=jt: e.tensor_tensor(
                            out=h_ar[:, jt, th * 512:(th + 1) * 512], in0=gq[:, th, :],
                            in1=pu[:, :], op=ALU.mult),
                            extra_waits=[("S", sv), ("T", tu)])
                        consumed(pu, "V", vv)
                        gq_read[th] = vv
                        lastv = vv
                    wb.mark_free(gslab, "S", P.semv["S"])
                    wb.mark_free(uslab, "V", lastv)
                return P.semv["V"]

            def down_proj(wsrc, nj, gate_tile_scalar, const_scale, h_done):
                for fg in range(4):
                    slab = wb.load(wsrc(fg), nj, 512)
                    dvw = wb.view(slab)
                    lastv = 0
                    for tt in range(NT):
                        ps = pb[rot["d"] % 6]; rot["d"] += 1
                        tvv = mmg(ps, ps[:, :],
                                  [(h_ar[:, jc, tt * 128:(tt + 1) * 128], dvw[:, jc, :])
                                   for jc in range(nj)],
                                  waits=[wb.ready[slab]] +
                                        ([("V", h_done)] if (fg == 0 and tt == 0)
                                         else []))
                        sc_ap = gate_tile_scalar(tt) if gate_tile_scalar else None
                        lastv = evac_add(tt, fg, ps, ps[:, :], tvv, scalar_ap=sc_ap,
                                         const=const_scale)
                    wb.mark_free(slab, "V", lastv)

            for jq in range(4):
                hv = up_and_h(
                    lambda jt, jq=jq: sh_up[:, (jq * 11 + jt) * 128:
                                            (jq * 11 + jt + 1) * 128]
                    .rearrange("(c p) w -> p c w", p=128),
                    lambda jt, jq=jq: sh_up[:, JSH + (jq * 11 + jt) * 128:
                                            JSH + (jq * 11 + jt + 1) * 128]
                    .rearrange("(c p) w -> p c w", p=128),
                    11)
                down_proj(
                    lambda fg, jq=jq: sh_dn[jq * 1408:(jq + 1) * 1408,
                                            fg * 512:(fg + 1) * 512]
                    .rearrange("(c p) w -> p c w", p=128),
                    11, None, 0.5, hv)

            offbb = sb(None, [128, 2, NE], st=pctx)
            scat = sb(None, [128, 16, 2], st=pctx)
            slotu = sb(None, [128, 16], mybir.dt.uint32, st=pctx)
            slotf = sb(None, [128, 1], st=pctx)
            posl = sb(None, [128, 1], st=pctx)
            selx = sb(None, [128, 1], st=pctx)
            tv = mmg(pb[6], pb[6][0:1, 0:NE], [(offa[:], cg_sb[:])], waits=[("dC", dv2)])
            vv = P.vop(lambda e: e.tensor_copy(out=offr[0:1, 0, :], in_=pb[6][0:1, 0:NE]),
                       extra_waits=[("T", tv)])
            consumed(pb[6], "V", vv)
            tv = mmg(pb[7], pb[7][0:1, 0:NE], [(offb[:], cg_sb[:])])
            vv = P.vop(lambda e: e.tensor_copy(out=offr[0:1, 1, :], in_=pb[7][0:1, 0:NE]),
                       extra_waits=[("T", tv)])
            consumed(pb[7], "V", vv)
            for half in range(2):
                tv = mmg(pb[6 + half], pb[6 + half][:, 0:NE],
                         [(onesrowf[:], offr[0:1, half, :])],
                         waits=[("V", P.semv["V"])])
                vv = P.vop(lambda e, half=half: e.tensor_copy(
                    out=offbb[:, half, :], in_=pb[6 + half][:, 0:NE]),
                    extra_waits=[("T", tv)])
                consumed(pb[6 + half], "V", vv)
            pfv = P.dma("sync", tokws_d, padfill[:, :], sem="dF")
            scv = 0
            for tt in range(NT):
                half = 0 if tt < 4 else 1
                pz = pb[6 + tt % 2]
                tv = mmg(pz, pz[:, 0:NE], [(onesrowf[:], run[0:1, tt, :])],
                         waits=[("V", P.semv["V"])])
                vv = P.vop(lambda e, tt=tt, pz=pz: e.tensor_tensor(
                    out=xg[:], in0=xloc[:, tt, :], in1=pz[:, 0:NE], op=ALU.add),
                    extra_waits=[("T", tv)])
                consumed(pz, "V", vv)
                for k, (ca, cwx) in enumerate([(c1a, cw1), (c2a, cw2)]):
                    si = tt * 2 + k
                    P.vop(lambda e, tt=tt, ca=ca: e.tensor_tensor(
                        out=t8[:], in0=xg[:], in1=ca[:, tt, :], op=ALU.mult))
                    P.vop(lambda e: e.tensor_reduce(out=posl[:], in_=t8[:], axis=AX.X,
                                                    op=ALU.add))
                    P.vop(lambda e, tt=tt, ca=ca, half=half: e.tensor_tensor(
                        out=t8[:], in0=offbb[:, half, :], in1=ca[:, tt, :],
                        op=ALU.mult))
                    P.vop(lambda e: e.tensor_reduce(out=pos[:], in_=t8[:], axis=AX.X,
                                                    op=ALU.add))
                    P.vop(lambda e: e.tensor_tensor(out=pos[:], in0=pos[:], in1=posl[:],
                                                    op=ALU.add))
                    P.vop(lambda e: e.tensor_scalar(out=vld[:], in0=pos[:], scalar1=CAP,
                                                    scalar2=None, op0=ALU.is_lt))
                    P.vop(lambda e, tt=tt, cwx=cwx, si=si: e.tensor_scalar(
                        out=scat[:, si, 0:1], in0=cwx[:, tt:tt + 1],
                        scalar1=vld[:, 0:1], scalar2=0.5, op0=ALU.mult, op1=ALU.mult))
                    P.vop(lambda e, tt=tt, si=si: e.tensor_copy(
                        out=scat[:, si, 1:2], in_=iotat[:, tt:tt + 1]))
                    P.vop(lambda e, tt=tt, ca=ca: e.tensor_tensor(
                        out=t8[:], in0=iota8[:], in1=ca[:, tt, :], op=ALU.mult))
                    P.vop(lambda e: e.tensor_reduce(out=selx[:], in_=t8[:], axis=AX.X,
                                                    op=ALU.add))
                    P.vop(lambda e: e.tensor_scalar(out=posl[:], in0=posl[:],
                                                    scalar1=float(CLOC - 1),
                                                    scalar2=None, op0=ALU.min))
                    P.vop(lambda e, si=si: e.scalar_tensor_tensor(
                        out=slotf[:], in0=selx[:], scalar=float(CLOC), in1=posl[:],
                        op0=ALU.mult, op1=ALU.add))
                    vv = P.vop(lambda e, si=si: e.tensor_copy(out=slotu[:, si:si + 1],
                                                              in_=slotf[:]))
                    scv = P.op("gpsimd", lambda e, si=si: e.indirect_dma_start(
                        out=tokws_d,
                        out_offset=bass.IndirectOffsetOnAxis(ap=slotu[:, si:si + 1],
                                                             axis=0),
                        in_=scat[:, si, :], in_offset=None),
                        incs=("dSC",),
                        waits=[("V", vv), ("dF", pfv), ("G", P.semv["G"])])

            wsld = sb(None, [128, NE * CLOC // 128], st=pctx)
            tokld = sb(None, [128, NE * CLOC // 128], st=pctx)
            sidxu = sb(None, [128, NE * CLOC // 128], mybir.dt.uint32, st=pctx)
            dwl = P.dma("sync", wsld[:, :],
                        tokws_d[:, 0:1].rearrange("(j p) o -> p (j o)", p=128),
                        waits=[("dSC", scv)], sem="dI", slow_ok=True)
            dtl = P.dma("sync", tokld[:, :],
                        tokws_d[:, 1:2].rearrange("(j p) o -> p (j o)", p=128),
                        waits=[("dI", dwl)], sem="dI", slow_ok=True)
            vsx = P.vop(lambda e: e.tensor_copy(out=sidxu[:, :], in_=tokld[:, :]),
                        extra_waits=[("dI", dtl)])
            P.wait("vector", "dI", dwl)

            P.wait("sync", "V", P.semv["V"])
            rsv = P.dma("sync", rdst, r_res[:, :, :], sem="dV0")

            rout_ctx.close()
            psT0 = ps_new(pctx, [128, 1024], BF16)
            psT1 = ps_new(pctx, [128, 1024], BF16)
            psfree[id(psT0)] = ("V", P.semv["V"])
            psfree[id(psT1)] = ("V", P.semv["V"])
            buf = sb(None, [128, DC, CLOC], BF16, st=pctx)
            buf_tok = sb(None, [128, D], BF16, st=pctx)
            h_e = h_ar[:, :, 0:CLOC]
            o_sb = sb(None, [128, 2, 512], st=pctx)
            gq2 = sb(None, [128, 512], BF16, st=pctx)
            buf_free = ("T", 0)
            btok_free = ("T", 0)
            gav = 0
            sadd = scv
            for ex in range(NE):
                lastTR = 0
                for stile in range(4):
                    wcol = ex * 4 + stile
                    gav = P.op("gpsimd", lambda e, wcol=wcol: e.indirect_dma_start(
                        out=buf_tok[:, :],
                        in_=xh_d,
                        in_offset=bass.IndirectOffsetOnAxis(
                            ap=sidxu[:, wcol:wcol + 1], axis=0),
                        out_offset=None),
                        incs=("dGA",),
                        waits=[btok_free, ("dX", xdv), ("G", P.semv["G"])] +
                              ([("V", vsx)] if wcol == 0 else []))
                    for c in range(DC):
                        pst = psT0 if c % 2 == 0 else psT1
                        tvt = P.top(lambda e, c=c, pst=pst: e.transpose(
                            pst[:, 0:128], buf_tok[:, c * 128:(c + 1) * 128],
                            ident[:]),
                            incs=("T",),
                            waits=([("dGA", gav)] if c == 0 else []) +
                                  [psfree[id(pst)]])
                        sv2 = P.sop(lambda e, c=c, pst=pst, stile=stile: e.copy(
                            out=buf[:, c, stile * 128:(stile + 1) * 128],
                            in_=pst[:, 0:128]),
                            extra_waits=[("T", tvt)] +
                            ([buf_free] if (stile == 0 and c == 0) else []))
                        consumed(pst, "S", sv2)
                        lastTR = tvt
                    btok_free = ("T", lastTR)
                buf_ready = P.semv["S"]
                gq_r = 0
                lastT = 0
                for jt in range(11):
                    gslab = wb.load(w_upg[ex, :, jt * 128:(jt + 1) * 128]
                                    .rearrange("(c p) w -> p c w", p=128), DC, 128)
                    uslab = wb.load(w_upg[ex, :, IEXP + jt * 128:IEXP + (jt + 1) * 128]
                                    .rearrange("(c p) w -> p c w", p=128), DC, 128)
                    gv = wb.view(gslab); uv = wb.view(uslab)
                    pg = pb[rot["g"] % 3]; rot["g"] += 1
                    pu = pb[3 + rot["u"] % 3]; rot["u"] += 1
                    tg = mmg(pg, pg[:, 0:CLOC],
                             [(gv[:, c, :], buf[:, c, :]) for c in range(DC)],
                             waits=[wb.ready[gslab]] +
                                   ([("S", buf_ready)] if jt == 0 else []))
                    tu = mmg(pu, pu[:, 0:CLOC],
                             [(uv[:, c, :], buf[:, c, :]) for c in range(DC)],
                             waits=[wb.ready[uslab]])
                    lastT = tu
                    sv = P.sop(lambda e, pg=pg: e.activation(
                        out=gq2[:], in_=pg[:, 0:CLOC], func=AF.Silu),
                        extra_waits=[("T", tg), ("V", gq_r)])
                    consumed(pg, "S", sv)
                    vv = P.vop(lambda e, pu=pu, jt=jt: e.tensor_tensor(
                        out=h_ar[:, jt, 0:CLOC], in0=gq2[:], in1=pu[:, 0:CLOC],
                        op=ALU.mult),
                        extra_waits=[("S", sv), ("T", tu)])
                    consumed(pu, "V", vv)
                    gq_r = vv
                    wb.mark_free(gslab, "S", sv)
                    wb.mark_free(uslab, "V", vv)
                buf_free = ("T", lastT)
                h_done = P.semv["V"]
                for fg in range(4):
                    slab = wb.load(w_dn[ex, :, fg * 512:(fg + 1) * 512]
                                   .rearrange("(c p) w -> p c w", p=128), 11, 512)
                    dvw = wb.view(slab)
                    lastv = 0
                    for stile in range(4):
                        ps = pb[rot["d"] % 6]; rot["d"] += 1
                        tvv = mmg(ps, ps[:, :],
                                  [(h_ar[:, jc, stile * 128:(stile + 1) * 128],
                                    dvw[:, jc, :]) for jc in range(11)],
                                  waits=[wb.ready[slab]] +
                                        ([("V", h_done)] if (fg == 0 and stile == 0)
                                         else []))
                        par = stile % 2
                        wcol = ex * 4 + stile
                        vv = P.vop(lambda e, ps=ps, par=par, wcol=wcol: e.tensor_scalar(
                            out=o_sb[:, par, :], in0=ps[:, :],
                            scalar1=wsld[:, wcol:wcol + 1], scalar2=None,
                            op0=ALU.mult),
                            extra_waits=[("T", tvv), ("dSC", sadd if True else 0),
                                         ("V", vsx if (ex == 0 and fg == 0 and
                                                       stile == 0) else 0)])
                        consumed(ps, "V", vv)
                        lastv = vv
                        sadd = P.op("gpsimd",
                                    lambda e, par=par, wcol=wcol, fg=fg:
                                    e.indirect_dma_start(
                                        out=rdst_rows,
                                        out_offset=bass.IndirectOffsetOnAxis(
                                            ap=sidxu[:, wcol:wcol + 1], axis=0),
                                        in_=o_sb[:, par, :], in_offset=None,
                                        element_offset=fg * 512,
                                        compute_op=ALU.add),
                                    incs=("dSC",),
                                    waits=[("V", vv), ("dV0", rsv),
                                           ("G", P.semv["G"])])
                    wb.mark_free(slab, "V", lastv)
            P.wait("sync", "dSC", P.semv["dSC"])
            P.barrier()
        P.barrier()

    def phase_mamba(r_src_d, r_dst_d):
        with ExitStack() as pctx:
            r_res = sb(None, [128, NT, D], st=pctx)
            rsrc = r_src_d.rearrange("(t p) d -> p t d", p=128)
            rdst = r_dst_d.rearrange("(t p) d -> p t d", p=128)
            ldv = P.dma("sync", r_res[:, :, :], rsrc, sem="dR0")
            with ExitStack() as lctx:
                xh_tok = sb(None, [128, NT, D], BF16, st=lctx)
                emit_ln2(lambda tt: (r_res[:, tt, :], ("dR0", ldv)), xh_tok, lctx)
                P.barrier()

            x_fm = sb(None, [128, 32, TLOC], BF16, st=pctx)
            w_grp = sb(None, [128, 8, TLOC], BF16, st=pctx)
            z_tmp = sb(None, [128, TLOC], BF16, st=pctx)
            wb = WBuf(P, "wb_mam", 2, 32 * 128, st=pctx)
            pb = [ps_new(pctx) for _ in range(8)]
            rx = [0]

            for it in range(32):
                slab = wb.load(w_in_t[:, it * 128:(it + 1) * 128]
                               .rearrange("(c p) w -> p c w", p=128), DC, 128)
                wvw = wb.view(slab)
                lastS = 0
                for th in range(2):
                    ps = pb[rx[0] % 3]; rx[0] += 1
                    tv = mmg(ps, ps[:, :],
                             [(wvw[:, c, :], xh_fm[:, c, th * 512:(th + 1) * 512])
                              for c in range(DC)],
                             waits=[wb.ready[slab]])
                    sv = P.sop(lambda e, ps=ps, it=it, th=th: e.copy(
                        out=x_fm[:, it, th * 512:(th + 1) * 512], in_=ps[:, :]),
                        extra_waits=[("T", tv)])
                    consumed(ps, "S", sv)
                    lastS = sv
                wb.mark_free(slab, "S", lastS)

            wgrp_free_T = 0
            for g in range(4):
                for ig in range(8):
                    i = g * 8 + ig
                    zslab = wb.load(w_in_t[:, INTER + i * 128:INTER + (i + 1) * 128]
                                    .rearrange("(c p) w -> p c w", p=128), DC, 128)
                    cslab = wb.load(conv_t[:, i * 128:(i + 1) * 128]
                                    .rearrange("(c p) w -> p c w", p=128), 32, 128)
                    zv = wb.view(zslab); cvw = wb.view(cslab)
                    lastS = 0
                    for th in range(2):
                        ps = pb[rx[0] % 3]; rx[0] += 1
                        tv = mmg(ps, ps[:, :],
                                 [(zv[:, c, :], xh_fm[:, c, th * 512:(th + 1) * 512])
                                  for c in range(DC)],
                                 waits=[wb.ready[zslab]])
                        sv = P.sop(lambda e, ps=ps, th=th: e.activation(
                            out=z_tmp[:, th * 512:(th + 1) * 512], in_=ps[:, :],
                            func=AF.Silu),
                            extra_waits=[("T", tv), ("V", P.semv["V"])])
                        consumed(ps, "S", sv)
                        lastS = sv
                    lastV = 0
                    for th in range(2):
                        ps = pb[3 + (ig * 2 + th) % 3]
                        tv = mmg(ps, ps[:, :],
                                 [(cvw[:, c, :], x_fm[:, c, th * 512:(th + 1) * 512])
                                  for c in range(32)],
                                 waits=[wb.ready[cslab], ("S", lastS),
                                        ("T", wgrp_free_T) if (ig == 0 and th == 0)
                                        else ("T", 0)])
                        vv = P.vop(lambda e, ps=ps, ig=ig, th=th, i=i:
                                   e.scalar_tensor_tensor(
                            out=w_grp[:, ig, th * 512:(th + 1) * 512], in0=ps[:, :],
                            scalar=dD[:, i:i + 1],
                            in1=z_tmp[:, th * 512:(th + 1) * 512],
                            op0=ALU.mult, op1=ALU.mult),
                            extra_waits=[("T", tv)] +
                            ([("T", wgrp_free_T)] if (ig == 0 and th == 0) else []))
                        consumed(ps, "V", vv)
                        lastV = vv
                    wb.mark_free(zslab, "S", lastS)
                    wb.mark_free(cslab, "V", lastV)
                for fg in range(4):
                    slab = wb.load(
                        w_out_t[g * 1024:(g + 1) * 1024, fg * 512:(fg + 1) * 512]
                        .rearrange("(c p) w -> p c w", p=128), 8, 512)
                    wvw = wb.view(slab)
                    lastv = 0
                    for tt in range(NT):
                        ps = [pb[6], pb[7], pb[0], pb[1]][tt % 4]
                        tv = mmg(ps, ps[:, :],
                                 [(w_grp[:, c, tt * 128:(tt + 1) * 128], wvw[:, c, :])
                                  for c in range(8)],
                                 waits=[wb.ready[slab], ("V", lastV)])
                        vv = P.vop(lambda e, ps=ps, tt=tt, fg=fg:
                                   e.scalar_tensor_tensor(
                            out=r_res[:, tt, fg * 512:(fg + 1) * 512], in0=ps[:, :],
                            scalar=1.0, in1=r_res[:, tt, fg * 512:(fg + 1) * 512],
                            op0=ALU.mult, op1=ALU.add), extra_waits=[("T", tv)])
                        consumed(ps, "V", vv)
                        lastv = vv
                    wb.mark_free(slab, "V", lastv)
                wgrp_free_T = P.semv["T"]

            P.barrier()
            P.dma("sync", rdst, r_res[:, :, :])
        P.barrier()

    l1_state = {"free": [("T", 0), ("T", 0)], "pb": None}

    def emit_l1_og(og, wv_ap, ready_wait, r_res):
        pb = l1_state["pb"]
        lastT = 0
        for tt in range(NT):
            ps = pb[tt % 3]
            w = ([ready_wait] if tt == 0 else []) + [psfree[id(ps)]]
            pairs = [(xh_fm[:, c, tt * 128:(tt + 1) * 128], wv_ap[:, c, :])
                     for c in range(DC)]
            for i, (l, r) in enumerate(pairs):
                P.top(lambda e, l=l, r=r, st=(i == 0), ps=ps: e.matmul(
                    ps[:, :], l, r, start=st, stop=False),
                    waits=w if i == 0 else ())
            tv = P.top(lambda e, ps=ps, og=og: e.matmul(
                ps[:, :], onesrowf[:], l1b[0:1, og * 512:(og + 1) * 512],
                start=False, stop=True), incs=("T",))
            vv = P.vop(lambda e, ps=ps, tt=tt, og=og: e.scalar_tensor_tensor(
                out=r_res[:, tt, og * 512:(og + 1) * 512], in0=ps[:, :], scalar=0.5,
                in1=r_res[:, tt, og * 512:(og + 1) * 512], op0=ALU.mult, op1=ALU.add),
                extra_waits=[("T", tv)])
            consumed(ps, "V", vv)
            lastT = tv
        l1_state["free"][og % 2] = ("T", lastT)

    def phase_l1ffn(r_src_d, r_dst_d, call_idx):
        with ExitStack() as pctx:
            r_res = sb(None, [128, NT, D], st=pctx)
            rsrc = r_src_d.rearrange("(t p) d -> p t d", p=128)
            ldv = P.dma("sync", r_res[:, :, :], rsrc, sem="dR0")
            with ExitStack() as lctx:
                xh_tok = sb(None, [128, NT, D], BF16, st=lctx)
                emit_ln2(lambda tt: (r_res[:, tt, :], ("dR0", ldv)), xh_tok, lctx)
                P.barrier()

            weff = sb(None, [128, 2, DC, 512], BF16, st=pctx)
            pb = [ps_new(pctx) for _ in range(4)]
            l1_state["free"] = [("T", 0), ("T", 0)]
            l1_state["pb"] = pb

            if call_idx == 0:
                wf = sb(None, [128, 2, DC, 256], st=pctx)
                acc = sb(None, [128, 1], st=pctx)
                scal = sb(None, [128, 4], st=pctx)
                srow = sb(None, [1, 4], st=pctx)
                tmp1 = sb(None, [128, 1], st=pctx)
                pos_t = sb(None, [128, DC, 256], BF16, st=pctx)
                neg_t = sb(None, [128, DC, 256], BF16, st=pctx)
                wstore = {0: 0, 1: 0}
                P.vop(lambda e: e.memset(acc[:], 0.0))
                for i in range(8):
                    dv = P.dma("gpsimd", wf[:, i % 2, :, :],
                               l1w_t[:, i * 256:(i + 1) * 256]
                               .rearrange("(c p) w -> p c w", p=128),
                               waits=[("V", P.semv["V"])] if i >= 2 else (),
                               sem=f"dW{i % 2}")
                    P.vop(lambda e, i=i: e.tensor_reduce(
                        out=tmp1[:], in_=wf[:, i % 2, :, :], axis=AX.XY, op=ALU.add,
                        apply_absolute_value=True), extra_waits=[(f"dW{i % 2}", dv)])
                    P.vop(lambda e: e.tensor_tensor(out=acc[:], in0=acc[:],
                                                    in1=tmp1[:], op=ALU.add))
                vv = P.semv["V"]
                tv = mmg(pb[3], pb[3][0:1, 0:1], [(acc[:], onescolf[:])],
                         waits=[("V", vv)])
                vv = P.vop(lambda e: e.tensor_scalar(out=srow[0:1, 0:1],
                                                     in0=pb[3][0:1, 0:1],
                                                     scalar1=1.0 / (D * D),
                                                     scalar2=None, op0=ALU.mult),
                           extra_waits=[("T", tv)])
                consumed(pb[3], "V", vv)
                P.vop(lambda e: e.tensor_scalar(out=srow[0:1, 1:2], in0=srow[0:1, 0:1],
                                                scalar1=1e-10, scalar2=0.5,
                                                op0=ALU.add, op1=ALU.mult))
                vv = P.vop(lambda e: e.tensor_scalar(out=srow[0:1, 2:3],
                                                     in0=srow[0:1, 1:2],
                                                     scalar1=-1.0, scalar2=None,
                                                     op0=ALU.mult))
                tv = mmg(pb[3], pb[3][:, 0:3], [(onesrowf[:], srow[0:1, 0:3])],
                         waits=[("V", vv)])
                vv = P.vop(lambda e: e.tensor_copy(out=scal[:, 0:3], in_=pb[3][:, 0:3]),
                           extra_waits=[("T", tv)])
                consumed(pb[3], "V", vv)
                for og in range(4):
                    for half in range(2):
                        j0 = og * 512 + half * 256
                        dv = P.dma("gpsimd", wf[:, half, :, :],
                                   l1w_t[:, j0:j0 + 256]
                                   .rearrange("(c p) w -> p c w", p=128),
                                   waits=[("V", P.semv["V"])], sem=f"dW{half}")
                        P.vop(lambda e, half=half: e.tensor_scalar(
                            out=pos_t[:], in0=wf[:, half, :, :], scalar1=scal[:, 1:2],
                            scalar2=None, op0=ALU.is_gt),
                            extra_waits=[(f"dW{half}", dv)])
                        P.vop(lambda e, half=half: e.tensor_scalar(
                            out=neg_t[:], in0=wf[:, half, :, :], scalar1=scal[:, 2:3],
                            scalar2=None, op0=ALU.is_lt))
                        P.vop(lambda e: e.tensor_tensor(out=pos_t[:], in0=pos_t[:],
                                                        in1=neg_t[:], op=ALU.subtract))
                        vv = P.vop(lambda e, og=og, half=half: e.tensor_scalar(
                            out=weff[:, og % 2, :, half * 256:(half + 1) * 256],
                            in0=pos_t[:], scalar1=scal[:, 0:1], scalar2=None,
                            op0=ALU.mult),
                            extra_waits=([l1_state["free"][og % 2],
                                          (f"dV{og % 2}", wstore[og % 2])]
                                         if half == 0 else []))
                    wstore[og % 2] = P.dma(
                        "sync", weff_d[:, og * 512:(og + 1) * 512]
                        .rearrange("(c p) w -> p c w", p=128),
                        weff[:, og % 2, :, :], waits=[("V", vv)], sem=f"dV{og % 2}")
                    emit_l1_og(og, weff[:, og % 2, :, :], ("V", vv), r_res)
            else:
                for og in range(4):
                    dv = P.dma("gpsimd", weff[:, og % 2, :, :],
                               weff_d[:, og * 512:(og + 1) * 512]
                               .rearrange("(c p) w -> p c w", p=128),
                               waits=[l1_state["free"][og % 2]], sem=f"dW{og % 2}")
                    emit_l1_og(og, weff[:, og % 2, :, :], (f"dW{og % 2}", dv), r_res)

            P.barrier()
            P.dma("sync", r_dst_d.rearrange("(t p) d -> p t d", p=128),
                  r_res[:, :, :])
        P.barrier()

    def phase_attn(r_src_d, r_dst_d):
        with ExitStack() as pctx:
            rsrc = r_src_d.rearrange("(t p) d -> p t d", p=128)
            rdst = r_dst_d.rearrange("(t p) d -> p t d", p=128)
            with ExitStack() as lctx:
                r_io = sb(None, [128, 2, D], st=lctx)
                xh_tok = sb(None, [128, NT, D], BF16, st=lctx)
                lastv = [0] * NT

                def rtile(tt):
                    par = tt % 2
                    dv = P.dma("sync", r_io[:, par, :], rsrc[:, tt, :],
                               waits=[("V", lastv[tt - 2])] if tt >= 2 else (),
                               sem=f"dR{par}")
                    return r_io[:, par, :], (f"dR{par}", dv)
                emit_ln2(lambda tt: rtile(tt), xh_tok, lctx, ln_apply_vs=lastv)
                P.barrier()

            wb = WBuf(P, "wb_attn", 2, 16 * 512, st=pctx)
            pb = [ps_new(pctx) for _ in range(8)]
            kv_sb = sb(None, [128, 2, 512], BF16, st=pctx)
            kvw = {0: 0, 1: 0}
            for dc in range(DC):
                slab = wb.load(wk[:, dc * 128:(dc + 1) * 128]
                               .rearrange("(c p) w -> p c w", p=128), DC, 128)
                wvw = wb.view(slab)
                lastS = 0
                for th in range(2):
                    ps = pb[th]
                    tv = mmg(ps, ps[:, :],
                             [(wvw[:, c, :], xh_fm[:, c, th * 512:(th + 1) * 512])
                              for c in range(DC)],
                             waits=[wb.ready[slab]])
                    sv = P.sop(lambda e, ps=ps, th=th: e.copy(
                        out=kv_sb[:, th, :], in_=ps[:, :]),
                        extra_waits=[("T", tv), (f"dV{th}", kvw[th])])
                    consumed(ps, "S", sv)
                    kvw[th] = P.dma(
                        "sync",
                        kv_loc[dc, 0:128 * TLOC].rearrange("(p w) -> p w", w=TLOC)[:, th * 512:(th + 1) * 512],
                        kv_sb[:, th, :], waits=[("S", sv)], sem=f"dV{th}")
                    lastS = sv
                wb.mark_free(slab, "S", lastS)
            for og in range(4):
                slab = wb.load(wv[:, og * 512:(og + 1) * 512]
                               .rearrange("(c p) w -> p c w", p=128), DC, 512)
                wvw = wb.view(slab)
                lastS = 0
                for tt in range(NT):
                    par = tt % 2
                    ps = pb[2 + par]
                    tv = mmg(ps, ps[:, :],
                             [(xh_fm[:, c, tt * 128:(tt + 1) * 128], wvw[:, c, :])
                              for c in range(DC)],
                             waits=[wb.ready[slab]])
                    sv = P.sop(lambda e, ps=ps, par=par: e.copy(
                        out=kv_sb[:, par, :], in_=ps[:, :]),
                        extra_waits=[("T", tv), (f"dV{par}", kvw[par])])
                    consumed(ps, "S", sv)
                    dv = 0
                    for hh in range(4):
                        h = og * 4 + hh
                        base = 128 * TLOC + tt * 128 * 128
                        dv = P.dma("sync",
                                   kv_loc[h, base:base + 128 * 128]
                                   .rearrange("(p w) -> p w", w=128),
                                   kv_sb[:, par, hh * 128:(hh + 1) * 128],
                                   waits=[("S", sv)], sem=f"dV{par}")
                    kvw[par] = dv
                    lastS = sv
                wb.mark_free(slab, "S", lastS)

            P.wait("gpsimd", "dV0", P.semv["dV0"])
            P.wait("gpsimd", "dV1", P.semv["dV1"])
            cv = P.op("gpsimd", lambda e: e.collective_compute(
                "AllGather", ALU.bypass, replica_groups=[[0, 1, 2, 3], [4, 5, 6, 7]],
                ins=[kv_loc.opt()], outs=[kv_gath.opt()]),
                incs=("C",), waits=[("G", P.semv["G"])])

            attnout = sb(None, [128, DC, TLOC], BF16, st=pctx)
            midctx = ExitStack()
            k_head = sb(None, [128, 32, 128], BF16, st=midctx)
            v_head = sb(None, [128, 32, 128], BF16, st=midctx)
            q_head = sb(None, [128, 512], BF16, st=midctx)
            p_sb = sb(None, [128, 4, 512], BF16, st=midctx)
            ms_sb = sb(None, [128, 4, 512], st=midctx)
            rz = sb(None, [1, 512], st=midctx)
            rzbc = sb(None, [128, 512], st=midctx)
            mask_sb = sb(None, [128, 32, 512], mybir.dt.uint8, st=midctx)

            kv_free_T = 0
            q_free = ("S", 0)
            for qt in range(2):
                mq = P.dma("sync", mask_sb[:, :, :],
                           mask_in[qt].rearrange("s p w -> p s w"),
                           waits=[("V", P.semv["V"])], sem="dM")
                for h in range(HEADS):
                    kds = 0
                    for g in range(8):
                        a = g if g < 4 else 7 - g
                        half = 0 if g < 4 else 1
                        P.dma("sync", k_head[:, g * 4:(g + 1) * 4, :],
                              kv_gath[a, h, 0:128 * TLOC]
                              .rearrange("(p w) -> p w", w=TLOC)
                              [:, half * 512:(half + 1) * 512]
                              .rearrange("p (s w) -> p s w", w=128),
                              waits=[("C", cv), ("T", kv_free_T)], sem="dK")
                        base = 128 * TLOC + half * 512 * 128
                        kds = P.dma("sync", v_head[:, g * 4:(g + 1) * 4, :],
                                    kv_gath[a, h, base:base + 512 * 128]
                                    .rearrange("(s p w) -> p s w", p=128, w=128),
                                    waits=[("C", cv), ("T", kv_free_T)], sem="dK")
                    qslab = wb.load(wq[:, h * 128:(h + 1) * 128]
                                    .rearrange("(c p) w -> p c w", p=128), DC, 128)
                    qvw = wb.view(qslab)
                    ps = pb[6]
                    tv = mmg(ps, ps[:, :],
                             [(qvw[:, c, :], xh_fm[:, c, qt * 512:(qt + 1) * 512])
                              for c in range(DC)],
                             waits=[wb.ready[qslab], q_free])
                    sv = P.sop(lambda e, ps=ps: e.copy(out=q_head[:], in_=ps[:, :]),
                               extra_waits=[("T", tv)])
                    consumed(ps, "S", sv)
                    q_free = ("S", sv)
                    wb.mark_free(qslab, "S", sv)

                    psO = pb[4 + h % 2]
                    psZa = pb[7]
                    lastZT = 0
                    lastPV = 0
                    pv_done = {i: ("T", 0) for i in range(4)}
                    exp_done = {i: 0 for i in range(4)}
                    sc_t = [0] * 32

                    def emit_scores(st, first_waits=()):
                        par = st % 4
                        psS = pb[par]
                        sc_t[st] = mmg(psS, psS[:, :],
                                       [(k_head[:, st, :], q_head[:])],
                                       waits=list(first_waits))
                    emit_scores(0, [("dK", kds), ("S", sv), ("dM", mq)])
                    for st in range(1, 4):
                        emit_scores(st)
                    for st in range(32):
                        par = st % 4
                        psS = pb[par]
                        vv = P.vop(lambda e, st=st, par=par, psS=psS: e.tensor_tensor(
                            out=ms_sb[:, par, :], in0=psS[:, :],
                            in1=mask_sb[:, st, :], op=ALU.mult),
                            extra_waits=[("T", sc_t[st]), ("S", exp_done[par])])
                        consumed(psS, "V", vv)
                        se = P.sop(lambda e, par=par: e.activation(
                            out=p_sb[:, par, :], in_=ms_sb[:, par, :], func=AF.Exp,
                            scale=1.0 / 64.0),
                            extra_waits=[("V", vv), pv_done[par]])
                        exp_done[par] = se
                        tz = P.top(lambda e, st=st: e.matmul(
                            psZa[0:1, :], onescol[:], p_sb[:, st % 4, :],
                            start=(st == 0), stop=(st == 31)),
                            incs=("T",) if st == 31 else (),
                            waits=[("S", se)] +
                                  ([psfree[id(psZa)]] if st == 0 else []))
                        tpv = P.top(lambda e, psO=psO, st=st: e.matmul(
                            psO[:, :], v_head[:, st, :], p_sb[:, st % 4, :],
                            start=(st == 0), stop=(st == 31)),
                            incs=("T",),
                            waits=[psfree[id(psO)]] if st == 0 else ())
                        pv_done[par] = ("T", tpv)
                        if st + 4 < 32:
                            emit_scores(st + 4)
                        if st == 31:
                            lastZT = tz
                            lastPV = tpv
                    vv = P.vop(lambda e: e.reciprocal(out=rz[0:1, :], in_=psZa[0:1, :]),
                               extra_waits=[("T", lastZT)])
                    consumed(psZa, "V", vv)
                    tb = mmg(pb[6], pb[6][:, :], [(onesrowf[:], rz[0:1, :])],
                             waits=[("V", vv)])
                    vv = P.vop(lambda e: e.tensor_copy(out=rzbc[:], in_=pb[6][:, :]),
                               extra_waits=[("T", tb)])
                    consumed(pb[6], "V", vv)
                    vv = P.vop(lambda e, h=h, qt=qt, psO=psO: e.tensor_tensor(
                        out=attnout[:, h, qt * 512:(qt + 1) * 512], in0=psO[:, :],
                        in1=rzbc[:], op=ALU.mult), extra_waits=[("T", lastPV)])
                    consumed(psO, "V", vv)
                    kv_free_T = P.semv["T"]

            P.barrier()
            midctx.close()
            attn_y = sb(None, [128, NT, D], BF16, st=pctx)
            r_io = sb(None, [128, 2, D], st=pctx)
            for og in range(4):
                slab = wb.load(wo[:, og * 512:(og + 1) * 512]
                               .rearrange("(c p) w -> p c w", p=128), DC, 512)
                wvw = wb.view(slab)
                lastv = 0
                for tt in range(NT):
                    ps = pb[tt % 3]
                    tv = mmg(ps, ps[:, :],
                             [(attnout[:, c, tt * 128:(tt + 1) * 128], wvw[:, c, :])
                              for c in range(DC)],
                             waits=[wb.ready[slab]])
                    vv = P.vop(lambda e, ps=ps, tt=tt, og=og: e.tensor_copy(
                        out=attn_y[:, tt, og * 512:(og + 1) * 512], in_=ps[:, :]),
                        extra_waits=[("T", tv)])
                    consumed(ps, "V", vv)
                    lastv = vv
                wb.mark_free(slab, "V", lastv)
            stored = {0: 0, 1: 0}
            for tt in range(NT):
                par = tt % 2
                dv = P.dma("sync", r_io[:, par, :], rsrc[:, tt, :],
                           waits=[(f"dV{par}", stored[par]),
                                  ("V", P.semv["V"] if tt < 2 else 0)],
                           sem=f"dR{par}")
                vv = P.vop(lambda e, tt=tt, par=par: e.tensor_tensor(
                    out=r_io[:, par, :], in0=r_io[:, par, :], in1=attn_y[:, tt, :],
                    op=ALU.add), extra_waits=[(f"dR{par}", dv)])
                stored[par] = P.dma("sync", rdst[:, tt, :], r_io[:, par, :],
                                    waits=[("V", vv)], sem=f"dV{par}")
        P.barrier()

    phases = [
        lambda s, d: phase_moe(s, d, 0),
        lambda s, d: phase_mamba(s, d),
        lambda s, d: phase_moe(s, d, 1),
        lambda s, d: phase_l1ffn(s, d, 0),
        lambda s, d: phase_attn(s, d),
        lambda s, d: phase_l1ffn(s, d, 1),
    ][:n_phases]
    src_ap = x0
    for i, ph in enumerate(phases):
        dst = out_d if i == len(phases) - 1 else r_d[i]
        ph(src_ap, dst)
        src_ap = dst

    P.wait("sync", "dS", P.semv["dS"])
    P.finalize()
    ctx.close()
    return nc


_PROG_CACHE = {}


def _get_prog():
    key = ("nc", _N_PHASES)
    if key not in _PROG_CACHE:
        _PROG_CACHE[key] = build_program(_N_PHASES)
    return _PROG_CACHE[key]


def _padfill():
    pf = np.zeros((128, 64), np.float32)
    for p in range(128):
        for jj in range(32):
            pf[p, 2 * jj + 1] = float((p * 32 + jj) % 128)
    return pf


def _consts():
    ident = np.eye(128, dtype=BF16NP)
    u128 = (np.arange(128)[:, None] < np.arange(128)[None, :]).astype(BF16NP)
    return dict(
        ident_in=ident, u128_in=u128,
        onescol_in=np.ones((128, 1), dtype=BF16NP),
        onescolf_in=np.ones((128, 1), dtype=np.float32),
        onesrowf_in=np.ones((1, 128), dtype=np.float32),
        iota8_in=np.broadcast_to(np.arange(8, dtype=np.float32), (128, 8)).copy(),
        iota9_in=np.broadcast_to(np.arange(9, 17, dtype=np.float32), (128, 8)).copy(),
        iotat_in=(np.arange(8)[None, :] * 128 +
                  np.arange(128)[:, None]).astype(np.float32),
        padfill_in=_padfill(),
        big8_in=np.full((128, 8), 8.0, dtype=np.float32),
        neg8_in=np.full((128, 8), -1e30, dtype=np.float32),
        onestile_in=np.ones((128, 512), dtype=BF16NP),
        epscol_in=np.full((128, 1), 1e-5, dtype=np.float32),
    )


def kernel(**inputs):
    x = np.asarray(inputs["L0_attn_hidden_states"], dtype=np.float32)
    shared = dict(
        w_in_t=np.ascontiguousarray(np.asarray(inputs["L0_attn_W_in"], np.float32).T),
        conv_t=np.ascontiguousarray(np.asarray(inputs["L0_attn_conv_w"], np.float32).T),
        w_out_t=np.ascontiguousarray(np.asarray(inputs["L0_attn_W_out"], np.float32).T),
        dvec=np.ascontiguousarray(
            np.asarray(inputs["L0_attn_D"], np.float32).reshape(32, 128).T),
        router=np.ascontiguousarray(np.asarray(inputs["L0_ffn_router"], np.float32)),
        w_upg=np.ascontiguousarray(np.asarray(inputs["L0_ffn_w_up_gate"], np.float32)),
        w_dn=np.ascontiguousarray(np.asarray(inputs["L0_ffn_w_down"], np.float32)),
        sh_up=np.ascontiguousarray(
            np.asarray(inputs["L0_ffn_shared_w_up_gate"], np.float32)),
        sh_dn=np.ascontiguousarray(
            np.asarray(inputs["L0_ffn_shared_w_down"], np.float32)),
        wq=np.ascontiguousarray(np.asarray(inputs["L1_attn_Wq"], np.float32)),
        wk=np.ascontiguousarray(np.asarray(inputs["L1_attn_Wk"], np.float32)),
        wv=np.ascontiguousarray(np.asarray(inputs["L1_attn_Wv"], np.float32)),
        wo=np.ascontiguousarray(np.asarray(inputs["L1_attn_Wo"], np.float32)),
        l1w_t=np.ascontiguousarray(np.asarray(inputs["L1_ffn_weight"], np.float32).T),
        l1b_row=np.ascontiguousarray(
            np.asarray(inputs["L1_ffn_bias"], np.float32).reshape(1, 2048)),
        **_consts(),
    )

    stripes = [(c // 4, c % 4, 7 - c % 4) for c in range(8)]

    def gstripe(rank, half):
        b, j = rank // 4, rank % 4
        return b * 8 + (j if half == 0 else 7 - j)

    in_maps = []
    for c in range(8):
        b, gA, gB = stripes[c]
        x0 = np.concatenate([x[b, gA * 512:(gA + 1) * 512],
                             x[b, gB * 512:(gB + 1) * 512]], axis=0)
        gAg, gBg = b * 8 + gA, b * 8 + gB
        offa = np.zeros((16, 1), np.float32)
        offb = np.zeros((16, 1), np.float32)
        for r2 in range(16):
            g = gstripe(r2 // 2, r2 % 2)
            if g < gAg:
                offa[r2, 0] = 1.0
            if g < gBg and r2 != c * 2:
                offb[r2, 0] = 1.0
        qoffs = [gA * 512, gB * 512]
        srange = np.arange(SEQ).reshape(32, 128)
        mask = np.zeros((2, 32, 128, 512), dtype=np.uint8)
        for qt in range(2):
            qglob = qoffs[qt] + np.arange(512)
            mask[qt] = (srange[:, :, None] <= qglob[None, None, :]).astype(np.uint8)
        in_maps.append(dict(x0=np.ascontiguousarray(x0), mask_in=mask,
                            offa_in=offa, offb_in=offb, **shared))

    nc = _get_prog()
    res = run_bass_kernel_spmd(nc, in_maps, core_ids=list(range(8)))

    out = np.zeros((2, 4096, 2048), dtype=np.float32)
    for c in range(8):
        b, gA, gB = stripes[c]
        o = np.asarray(res.results[c]["out"])
        out[b, gA * 512:(gA + 1) * 512] = o[:512]
        out[b, gB * 512:(gB + 1) * 512] = o[512:]
    return out


# version 08 (final)
# speedup vs baseline: 1.0723x; 1.0723x over previous
import numpy as np
import ml_dtypes
from contextlib import ExitStack

import concourse.bass as bass
import concourse.mybir as mybir
from concourse.bass_utils import run_bass_kernel_spmd

F32 = mybir.dt.float32
BF16 = mybir.dt.bfloat16
AF = mybir.ActivationFunctionType
ALU = mybir.AluOpType
AX = mybir.AxisListType
BF16NP = ml_dtypes.bfloat16

ENGINES = ("sync", "tensor", "vector", "scalar", "gpsimd")
SEM_OF = {"tensor": "T", "vector": "V", "scalar": "S", "gpsimd": "G"}
ALL_SEMS = ("T", "V", "S", "G", "C", "dS", "dR0", "dR1", "dK", "dM", "dC",
            "dV0", "dV1", "dG", "dW0", "dW1", "dW2", "dW3")
DMA_SEMS = ("dS", "dR0", "dR1", "dK", "dM", "dC", "dV0", "dV1", "dG",
            "dW0", "dW1", "dW2", "dW3")

D = 2048
DC = D // 128
TLOC = 1024
NT = TLOC // 128
INTER = 4096
NE, KTOP, IEXP, JSH = 8, 2, 1408, 5632
CAP = 2560.0
HEADS = 16
SEQ = 4096


class Prog:
    def __init__(self, nc, ctx):
        self.nc = nc
        self.ctx = ctx
        self.items = {e: [] for e in ENGINES}
        self.semv = {s: 0 for s in ALL_SEMS}
        self.sems = {s: ctx.enter_context(nc.semaphore("sem_" + s)) for s in ALL_SEMS}
        self.last = {e: None for e in ENGINES}
        self.dirty = set()

    def op(self, e, build, incs=(), waits=()):
        item = {"build": build, "incs": [], "waits": [w for w in waits if w[1] > 0]}
        post = []
        for s in incs:
            amt = 16 if s in DMA_SEMS else 1
            self.semv[s] += amt
            item["incs"].append((s, amt))
            post.append(self.semv[s])
        self.items[e].append(item)
        self.last[e] = item
        self.dirty.add(e)
        return post[0] if len(post) == 1 else post

    def wait(self, e, s, v):
        if v <= 0:
            return
        self.items[e].append({"build": None, "incs": [], "waits": [(s, v)]})

    def vop(self, build, extra_waits=()):
        w = [("V", self.semv["V"])]
        return self.op("vector", build, incs=("V",), waits=list(extra_waits) + w)

    def sop(self, build, extra_waits=()):
        w = [("S", self.semv["S"])]
        return self.op("scalar", build, incs=("S",), waits=list(extra_waits) + w)

    def gop(self, build, extra_waits=()):
        w = [("G", self.semv["G"])]
        return self.op("gpsimd", build, incs=("G",), waits=list(extra_waits) + w)

    def top(self, build, incs=(), waits=()):
        return self.op("tensor", build, incs=incs, waits=waits)

    def dma(self, queue, out_ap, in_ap, waits=(), sem="dS"):
        return self.op(queue, lambda e, o=out_ap, i=in_ap: e.dma_start(out=o, in_=i),
                       incs=(sem,), waits=waits)

    def barrier(self):
        for e in list(self.dirty):
            it = self.last[e]
            if it is None or it["build"] is None:
                continue
            s = SEM_OF.get(e)
            if s is None:
                continue
            if any(x[0] in DMA_SEMS for x in it["incs"]):
                continue
            if any(x[0] == s for x in it["incs"]):
                continue
            self.semv[s] += 1
            it["incs"].append((s, 1))
        self.dirty.clear()
        wl = [(s, self.semv[s]) for s in ALL_SEMS if self.semv[s] > 0]
        for e in ENGINES:
            own = SEM_OF.get(e)
            for (s, v) in wl:
                if s == own:
                    continue
                self.wait(e, s, v)

    def finalize(self):
        block = self.ctx.enter_context(self.nc.Block())
        for e in ENGINES:
            items = self.items[e]

            def make(items=items):
                def fn(eng):
                    for it in items:
                        for (s, v) in it["waits"]:
                            eng.wait_ge(self.sems[s], v)
                        if it["build"] is not None:
                            ins = it["build"](eng)
                            for (s, amt) in it["incs"]:
                                ins.then_inc(self.sems[s], amt)
                return fn
            getattr(block, e)(make())


class WBuf:

    def __init__(self, P, name, nslots, elems, st=None):
        self.P = P
        self.nslots = nslots
        self.elems = elems
        self.t = (st or P.ctx).enter_context(
            P.nc.sbuf_tensor(name, [128, nslots * elems], BF16))
        self.free_at = [("T", 0)] * nslots
        self.ready = [0] * nslots
        self.shape = [None] * nslots
        self.rr = 0

    def load(self, dram_ap, C, W, cast=True):
        P = self.P
        slot = self.rr
        self.rr = (self.rr + 1) % self.nslots
        assert C * W <= self.elems, (C, W, self.elems)
        dst = self.t[:, slot * self.elems: slot * self.elems + C * W]
        dst = dst.rearrange("p (c w) -> p c w", w=W)
        sem, val = self.free_at[slot]
        self.ready[slot] = (f"dW{slot}",
                            P.dma("gpsimd", dst, dram_ap, waits=[(sem, val)],
                                  sem=f"dW{slot}"))
        self.shape[slot] = (C, W)
        return slot

    def view(self, slot):
        C, W = self.shape[slot]
        return self.t[:, slot * self.elems: slot * self.elems + C * W].rearrange(
            "p (c w) -> p c w", w=W)

    def mark_free(self, slot, sem, val):
        self.free_at[slot] = (sem, val)


_N_PHASES = 6


def build_program(n_phases=None):
    n_phases = _N_PHASES if n_phases is None else n_phases
    nc = bass.Bass("TRN2", target_bir_lowering=False, debug=False)
    ctx = ExitStack()
    P = Prog(nc, ctx)

    def dt(name, shape, dty=F32, kind="ExternalInput"):
        return nc.dram_tensor(name, shape, dty, kind=kind).ap()

    x0 = dt("x0", [TLOC, D])
    w_in_t = dt("w_in_t", [D, 2 * INTER])
    conv_t = dt("conv_t", [INTER, INTER])
    w_out_t = dt("w_out_t", [INTER, D])
    dvec = dt("dvec", [128, INTER // 128])
    router = dt("router", [D, NE])
    w_upg = dt("w_upg", [NE, D, 2 * IEXP])
    w_dn = dt("w_dn", [NE, IEXP, D])
    sh_up = dt("sh_up", [D, 2 * JSH])
    sh_dn = dt("sh_dn", [JSH, D])
    wq = dt("wq", [D, D]); wk = dt("wk", [D, D]); wv = dt("wv", [D, D]); wo = dt("wo", [D, D])
    l1w_t = dt("l1w_t", [D, D])
    l1b_row = dt("l1b_row", [1, D])
    mask_in = dt("mask_in", [2, 32, 128, 512], mybir.dt.uint8)
    offa_in = dt("offa_in", [16, 1])
    offb_in = dt("offb_in", [16, 1])
    ident_in = dt("ident_in", [128, 128], BF16)
    onescol_in = dt("onescol_in", [128, 1], BF16)
    onescolf_in = dt("onescolf_in", [128, 1])
    onesrowf_in = dt("onesrowf_in", [1, 128])
    iota8_in = dt("iota8_in", [128, NE])
    iota9_in = dt("iota9_in", [128, NE])
    big8_in = dt("big8_in", [128, NE])
    neg8_in = dt("neg8_in", [128, NE])
    onestile_in = dt("onestile_in", [128, 512], BF16)
    u128_in = dt("u128_in", [128, 128], BF16)
    epscol_in = dt("epscol_in", [128, 1])

    out_d = dt("out", [TLOC, D], F32, kind="ExternalOutput")

    r_d = [dt(f"rr{i}", [TLOC, D], F32, kind="Internal") for i in range(5)]
    kv_loc = dt("kv_loc", [HEADS, 2 * 128 * TLOC], BF16, kind="Internal")
    kv_gath = dt("kv_gath", [4, HEADS, 2 * 128 * TLOC], BF16, kind="Internal")
    cnt_loc = dt("cnt_loc", [2, NE], F32, kind="Internal")
    cnt_gath = dt("cnt_gath", [16, NE], F32, kind="Internal")
    weff_d = dt("weff_d", [D, D], BF16, kind="Internal")

    _name_ctr = [0]

    def sb(name, shape, dty=F32, st=ctx):
        if name is None:
            _name_ctr[0] += 1
            name = f"sbt{_name_ctr[0]}"
        return st.enter_context(nc.sbuf_tensor(name, shape, dty))

    def psum(name, shape, dty=F32):
        return ctx.enter_context(nc.psum_tensor(name, shape, dty))

    ident = sb("ident", [128, 128], BF16)
    u128 = sb("u128", [128, 128], BF16)
    onescol = sb("onescol", [128, 1], BF16)
    onescolf = sb("onescolf", [128, 1])
    onesrowf = sb("onesrowf", [1, 128])
    iota8 = sb("iota8", [128, NE])
    iota9 = sb("iota9", [128, NE])
    big8 = sb("big8", [128, NE])
    neg8 = sb("neg8", [128, NE])
    onestile = sb("onestile", [128, 512], BF16)
    offa = sb("offa", [16, 1])
    offb = sb("offb", [16, 1])
    dD = sb("dD", [128, INTER // 128])
    l1b = sb("l1b", [1, D])
    xh_fm = sb("xh_fm", [128, DC, TLOC], BF16)
    epscol = sb("epscol", [128, 1])

    psfree = {}
    _ps_refs = []
    _ps_ctr = [0]

    def ps_new(st, shape=None, dty=F32):
        _ps_ctr[0] += 1
        t = st.enter_context(nc.psum_tensor(f"ps{_ps_ctr[0]}",
                                            shape or [128, 512], dty))
        psfree[id(t)] = ("T", 0)
        _ps_refs.append(t)
        return t

    def mmg(ps_full, out_ap, pairs, waits=(), inc=True, track=True):
        w = list(waits)
        if track:
            w.append(psfree[id(ps_full)])
        n = len(pairs)
        tval = None
        for i, (lhsT, rhs) in enumerate(pairs):
            ww = w if i == 0 else []
            incs = ("T",) if (inc and i == n - 1) else ()
            r = P.top(lambda e, l=lhsT, rr=rhs, st=(i == 0), sp=(i == n - 1), o=out_ap:
                      e.matmul(o, l, rr, start=st, stop=sp), incs=incs, waits=ww)
            if incs:
                tval = r
        return tval

    def consumed(ps_full, sem, val):
        psfree[id(ps_full)] = (sem, val)

    for dst, src in [(ident, ident_in), (u128, u128_in), (onescol, onescol_in),
                     (onescolf, onescolf_in), (onesrowf, onesrowf_in),
                     (iota8, iota8_in), (iota9, iota9_in),
                     (big8, big8_in), (neg8, neg8_in),
                     (onestile, onestile_in), (offa, offa_in), (offb, offb_in),
                     (dD, dvec), (l1b, l1b_row), (epscol, epscol_in)]:
        P.dma("sync", dst[:], src)
    P.barrier()

    def emit_ln2(r_tile_fn, xh_tok, lctx, ln_apply_vs=None):
        psT0 = ps_new(lctx, [128, 1024], BF16)
        psT1 = ps_new(lctx, [128, 1024], BF16)
        stats = sb(None, [128, 4 * 6], st=lctx)
        aggr = sb(None, [128, 2], st=lctx)
        sd = sb(None, [128, 1], st=lctx)
        rs = sb(None, [128, 1], st=lctx)
        prev_sqrt_s = 0
        prev_recip_v = 0
        for tt in range(NT):
            rt, rwait = r_tile_fn(tt)
            for i in range(4):
                P.vop(lambda e, i=i, rt=rt: e.bn_stats(
                    out=stats[:, i * 6:(i + 1) * 6], in_=rt[:, i * 512:(i + 1) * 512]),
                    extra_waits=([rwait] if (i == 0 and rwait) else []))
            P.vop(lambda e: e.bn_aggr(out=aggr[:], in_=stats[:]),
                  extra_waits=[("S", prev_sqrt_s)])
            vv = P.semv["V"]
            prev_sqrt_s = P.sop(lambda e: e.activation(out=sd[:], in_=aggr[:, 1:2],
                                                       func=AF.Sqrt, bias=epscol[:, 0:1]),
                                extra_waits=[("V", vv), ("V", prev_recip_v)])
            prev_recip_v = P.vop(lambda e: e.reciprocal(out=rs[:], in_=sd[:]),
                                 extra_waits=[("S", prev_sqrt_s)])
            vA = P.vop(lambda e, tt=tt, rt=rt: e.tensor_scalar(
                out=xh_tok[:, tt, :], in0=rt, scalar1=aggr[:, 0:1], scalar2=rs[:, 0:1],
                op0=ALU.subtract, op1=ALU.mult))
            if ln_apply_vs is not None:
                ln_apply_vs[tt] = vA
            for g in range(DC // 4):
                pst = psT0 if g % 2 == 0 else psT1
                tv = 0
                for cc in range(4):
                    c = g * 4 + cc
                    tv = P.top(lambda e, tt=tt, c=c, cc=cc, pst=pst: e.transpose(
                        pst[:, cc * 128:(cc + 1) * 128],
                        xh_tok[:, tt, c * 128:(c + 1) * 128], ident[:]),
                        incs=("T",) if cc == 3 else (),
                        waits=(([("V", vA)] if g == 0 else []) + [psfree[id(pst)]])
                              if cc == 0 else [])
                sv = P.sop(lambda e, tt=tt, g=g, pst=pst: e.copy(
                    out=xh_fm[:, g * 4:(g + 1) * 4, tt * 128:(tt + 1) * 128],
                    in_=pst[:, 0:512].rearrange("p (c w) -> p c w", w=128)),
                    extra_waits=[("T", tv)])
                consumed(pst, "S", sv)

    def phase_moe(r_src_d, r_dst_d, call_idx):
        with ExitStack() as pctx:
            r_res = sb(None, [128, NT, D], st=pctx)
            rsrc = r_src_d.rearrange("(t p) d -> p t d", p=128)
            rdst = r_dst_d.rearrange("(t p) d -> p t d", p=128)
            ldv = P.dma("sync", r_res[:, :, :], rsrc, sem="dR0")
            with ExitStack() as lctx:
                xh_tok = sb(None, [128, NT, D], BF16, st=lctx)
                emit_ln2(lambda tt: (r_res[:, tt, :], ("dR0", ldv)), xh_tok, lctx)
                P.barrier()

            wb = WBuf(P, f"wb_moe{call_idx}", 4, 16 * 512, st=pctx)
            pb = [ps_new(pctx) for _ in range(8)]
            rot = {"g": 0, "u": 0, "d": 0}
            h_ar = sb(None, [128, 11, TLOC], BF16, st=pctx)
            rt_sb = sb(None, [128, DC, NE], BF16, st=pctx)
            lg = sb(None, [128, NT, NE], st=pctx)
            c1a = sb(None, [128, NT, NE], st=pctx)
            c2a = sb(None, [128, NT, NE], st=pctx)
            sbf = sb(None, [128, NE], BF16, st=pctx)
            xloc = sb(None, [128, NT, NE], st=pctx)
            xg = sb(None, [128, NE], st=pctx)
            gsc = sb(None, [128, NT, NE], st=pctx)
            run = sb(None, [1, NT + 1, NE], st=pctx)
            snap = sb(None, [1, 2, NE], st=pctx)
            cg_sb = sb(None, [16, NE], st=pctx)
            offr = sb(None, [1, 2, NE], st=pctx)
            obase = sb(None, [1, NE], st=pctx)
            dd = sb(None, [128, NT], st=pctx)
            m1 = sb(None, [128, 1], st=pctx); m2 = sb(None, [128, 1], st=pctx)
            eq = sb(None, [128, NE], st=pctx); cand = sb(None, [128, NE], st=pctx)
            idx = sb(None, [128, 1], st=pctx); lg2 = sb(None, [128, NE], st=pctx)
            cw1 = sb(None, [128, NT], st=pctx); cw2 = sb(None, [128, NT], st=pctx)
            pos = sb(None, [128, 1], st=pctx); vld = sb(None, [128, 1], st=pctx)
            wk1 = sb(None, [128, 1], st=pctx); wk2 = sb(None, [128, 1], st=pctx)
            t8 = sb(None, [128, NE], st=pctx)
            gq = sb(None, [128, 2, 512], BF16, st=pctx)

            rv = P.dma("gpsimd", rt_sb[:, :, :],
                       router.rearrange("(c p) e -> p c e", p=128), sem="dG")
            P.vop(lambda e: e.memset(run[0:1, 0, :], 0.0))
            for tt in range(NT):
                pairs = [(xh_fm[:, c, tt * 128:(tt + 1) * 128], rt_sb[:, c, :])
                         for c in range(DC)]
                tv = mmg(pb[5], pb[5][:, 0:NE], pairs,
                         waits=[("dG", rv)] if tt == 0 else ())
                vv = P.vop(lambda e, tt=tt: e.tensor_copy(out=lg[:, tt, :],
                                                          in_=pb[5][:, 0:NE]),
                           extra_waits=[("T", tv)])
                consumed(pb[5], "V", vv)
                P.vop(lambda e, tt=tt: e.tensor_reduce(out=m1[:], in_=lg[:, tt, :],
                                                       axis=AX.X, op=ALU.max))
                P.vop(lambda e, tt=tt: e.tensor_scalar(out=eq[:], in0=lg[:, tt, :],
                                                       scalar1=m1[:, 0:1], scalar2=None,
                                                       op0=ALU.is_equal))
                P.vop(lambda e: e.scalar_tensor_tensor(out=cand[:], in0=eq[:],
                                                       scalar=-9.0, in1=iota9[:],
                                                       op0=ALU.mult, op1=ALU.add))
                P.vop(lambda e: e.tensor_reduce(out=idx[:], in_=cand[:], axis=AX.X,
                                                op=ALU.min))
                P.vop(lambda e, tt=tt: e.tensor_scalar(out=c1a[:, tt, :], in0=iota8[:],
                                                       scalar1=idx[:, 0:1], scalar2=None,
                                                       op0=ALU.is_equal))
                P.vop(lambda e, tt=tt: e.scalar_tensor_tensor(
                    out=lg2[:], in0=c1a[:, tt, :], scalar=-1e30, in1=lg[:, tt, :],
                    op0=ALU.mult, op1=ALU.add))
                P.vop(lambda e: e.tensor_reduce(out=m2[:], in_=lg2[:], axis=AX.X,
                                                op=ALU.max))
                P.vop(lambda e: e.tensor_scalar(out=eq[:], in0=lg2[:], scalar1=m2[:, 0:1],
                                                scalar2=None, op0=ALU.is_equal))
                P.vop(lambda e: e.scalar_tensor_tensor(out=cand[:], in0=eq[:],
                                                       scalar=-9.0, in1=iota9[:],
                                                       op0=ALU.mult, op1=ALU.add))
                P.vop(lambda e: e.tensor_reduce(out=idx[:], in_=cand[:], axis=AX.X,
                                                op=ALU.min))
                P.vop(lambda e, tt=tt: e.tensor_scalar(out=c2a[:, tt, :], in0=iota8[:],
                                                       scalar1=idx[:, 0:1], scalar2=None,
                                                       op0=ALU.is_equal))
                P.vop(lambda e, tt=tt: e.tensor_tensor(out=dd[:, tt:tt + 1], in0=m1[:],
                                                       in1=m2[:], op=ALU.subtract))
                P.vop(lambda e, tt=tt: e.tensor_tensor(out=eq[:], in0=c1a[:, tt, :],
                                                       in1=c2a[:, tt, :], op=ALU.add))
                vS = P.vop(lambda e: e.tensor_copy(out=sbf[:], in_=eq[:]))
                tv2 = mmg(pb[6], pb[6][:, 0:NE], [(u128[:], sbf[:])], waits=[("V", vS)])
                tv3 = mmg(pb[7], pb[7][0:1, 0:NE], [(onescol[:], sbf[:])], waits=[])
                vv = P.vop(lambda e, tt=tt: e.tensor_copy(out=xloc[:, tt, :],
                                                          in_=pb[6][:, 0:NE]),
                           extra_waits=[("T", tv2)])
                consumed(pb[6], "V", vv)
                vv = P.vop(lambda e, tt=tt: e.tensor_tensor(out=run[0:1, tt + 1, :],
                                                            in0=run[0:1, tt, :],
                                                            in1=pb[7][0:1, 0:NE],
                                                            op=ALU.add),
                           extra_waits=[("T", tv3)])
                consumed(pb[7], "V", vv)
            P.vop(lambda e: e.tensor_copy(out=snap[0:1, 0, :], in_=run[0:1, 4, :]))
            vSn = P.vop(lambda e: e.tensor_tensor(out=snap[0:1, 1, :],
                                                  in0=run[0:1, NT, :],
                                                  in1=run[0:1, 4, :], op=ALU.subtract))
            dv = P.dma("sync", cnt_loc, snap[0:1, :, :], waits=[("V", vSn)], sem="dC")
            cv = P.op("gpsimd", lambda e: e.collective_compute(
                "AllGather", ALU.bypass, replica_groups=[list(range(8))],
                ins=[cnt_loc.opt()], outs=[cnt_gath.opt()]),
                incs=("C",), waits=[("dC", dv), ("G", P.semv["G"])])
            dv2 = P.dma("sync", cg_sb[:], cnt_gath, waits=[("C", cv)], sem="dC")
            svw = P.sop(lambda e: e.activation(out=cw1[:, 0:NT], in_=dd[:, 0:NT],
                                               func=AF.Sigmoid),
                        extra_waits=[("V", P.semv["V"])])
            P.vop(lambda e: e.tensor_scalar(out=cw2[:, 0:NT], in0=cw1[:, 0:NT],
                                            scalar1=-1.0, scalar2=1.0, op0=ALU.mult,
                                            op1=ALU.add), extra_waits=[("S", svw)])

            def evac_add(tt, fg, ps_full, ps_ap, tval, scalar_ap=None, const=None):
                rslice = r_res[:, tt, fg * 512:(fg + 1) * 512]
                s = scalar_ap if scalar_ap is not None else const
                vv = P.vop(lambda e, r=rslice, p=ps_ap, s=s: e.scalar_tensor_tensor(
                    out=r, in0=p, scalar=s, in1=r, op0=ALU.mult, op1=ALU.add),
                    extra_waits=[("T", tval)])
                consumed(ps_full, "V", vv)
                return vv

            gq_read = {0: 0, 1: 0}

            def up_and_h(gate_src, up_src, nj):
                for jt in range(nj):
                    gslab = wb.load(gate_src(jt), DC, 128)
                    uslab = wb.load(up_src(jt), DC, 128)
                    gv = wb.view(gslab); uv = wb.view(uslab)
                    lastv = 0
                    for th in range(2):
                        pg = pb[rot["g"] % 3]; rot["g"] += 1
                        pu = pb[3 + rot["u"] % 3]; rot["u"] += 1
                        tg = mmg(pg, pg[:, :],
                                 [(gv[:, c, :], xh_fm[:, c, th * 512:(th + 1) * 512])
                                  for c in range(DC)],
                                 waits=[wb.ready[gslab]])
                        tu = mmg(pu, pu[:, :],
                                 [(uv[:, c, :], xh_fm[:, c, th * 512:(th + 1) * 512])
                                  for c in range(DC)],
                                 waits=[wb.ready[uslab]])
                        sv = P.sop(lambda e, pg=pg, th=th: e.activation(
                            out=gq[:, th, :], in_=pg[:, :], func=AF.Silu),
                            extra_waits=[("T", tg), ("V", gq_read[th])])
                        consumed(pg, "S", sv)
                        vv = P.vop(lambda e, pu=pu, th=th, jt=jt: e.tensor_tensor(
                            out=h_ar[:, jt, th * 512:(th + 1) * 512], in0=gq[:, th, :],
                            in1=pu[:, :], op=ALU.mult),
                            extra_waits=[("S", sv), ("T", tu)])
                        consumed(pu, "V", vv)
                        gq_read[th] = vv
                        lastv = vv
                    wb.mark_free(gslab, "S", P.semv["S"])
                    wb.mark_free(uslab, "V", lastv)
                return P.semv["V"]

            def down_proj(wsrc, nj, gate_tile_scalar, const_scale, h_done):
                for fg in range(4):
                    slab = wb.load(wsrc(fg), nj, 512)
                    dvw = wb.view(slab)
                    lastv = 0
                    for tt in range(NT):
                        ps = pb[rot["d"] % 6]; rot["d"] += 1
                        tvv = mmg(ps, ps[:, :],
                                  [(h_ar[:, jc, tt * 128:(tt + 1) * 128], dvw[:, jc, :])
                                   for jc in range(nj)],
                                  waits=[wb.ready[slab]] +
                                        ([("V", h_done)] if (fg == 0 and tt == 0)
                                         else []))
                        sc_ap = gate_tile_scalar(tt) if gate_tile_scalar else None
                        lastv = evac_add(tt, fg, ps, ps[:, :], tvv, scalar_ap=sc_ap,
                                         const=const_scale)
                    wb.mark_free(slab, "V", lastv)

            for jq in range(4):
                hv = up_and_h(
                    lambda jt, jq=jq: sh_up[:, (jq * 11 + jt) * 128:
                                            (jq * 11 + jt + 1) * 128]
                    .rearrange("(c p) w -> p c w", p=128),
                    lambda jt, jq=jq: sh_up[:, JSH + (jq * 11 + jt) * 128:
                                            JSH + (jq * 11 + jt + 1) * 128]
                    .rearrange("(c p) w -> p c w", p=128),
                    11)
                down_proj(
                    lambda fg, jq=jq: sh_dn[jq * 1408:(jq + 1) * 1408,
                                            fg * 512:(fg + 1) * 512]
                    .rearrange("(c p) w -> p c w", p=128),
                    11, None, 0.5, hv)

            tv = mmg(pb[6], pb[6][0:1, 0:NE], [(offa[:], cg_sb[:])], waits=[("dC", dv2)])
            vv = P.vop(lambda e: e.tensor_copy(out=offr[0:1, 0, :], in_=pb[6][0:1, 0:NE]),
                       extra_waits=[("T", tv)])
            consumed(pb[6], "V", vv)
            tv = mmg(pb[7], pb[7][0:1, 0:NE], [(offb[:], cg_sb[:])])
            vv = P.vop(lambda e: e.tensor_copy(out=offr[0:1, 1, :], in_=pb[7][0:1, 0:NE]),
                       extra_waits=[("T", tv)])
            consumed(pb[7], "V", vv)
            for tt in range(NT):
                half = 0 if tt < 4 else 1
                vv = P.vop(lambda e, tt=tt, half=half: e.tensor_tensor(
                    out=obase[0:1, :], in0=run[0:1, tt, :], in1=offr[0:1, half, :],
                    op=ALU.add))
                pz = pb[6 + tt % 2]
                tv = mmg(pz, pz[:, 0:NE], [(onesrowf[:], obase[0:1, :])],
                         waits=[("V", vv)])
                vv = P.vop(lambda e, tt=tt, pz=pz: e.tensor_tensor(out=xg[:],
                                                            in0=xloc[:, tt, :],
                                                            in1=pz[:, 0:NE],
                                                            op=ALU.add),
                           extra_waits=[("T", tv)])
                consumed(pz, "V", vv)
                for k, (ca, cwx, wkx) in enumerate([(c1a, cw1, wk1), (c2a, cw2, wk2)]):
                    P.vop(lambda e, tt=tt, ca=ca: e.tensor_tensor(
                        out=t8[:], in0=xg[:], in1=ca[:, tt, :], op=ALU.mult))
                    P.vop(lambda e: e.tensor_reduce(out=pos[:], in_=t8[:], axis=AX.X,
                                                    op=ALU.add))
                    P.vop(lambda e: e.tensor_scalar(out=vld[:], in0=pos[:], scalar1=CAP,
                                                    scalar2=None, op0=ALU.is_lt))
                    P.vop(lambda e, tt=tt, cwx=cwx, wkx=wkx: e.tensor_scalar(
                        out=wkx[:], in0=cwx[:, tt:tt + 1], scalar1=vld[:, 0:1],
                        scalar2=0.5, op0=ALU.mult, op1=ALU.mult))
                P.vop(lambda e, tt=tt: e.tensor_scalar(out=t8[:], in0=c1a[:, tt, :],
                                                       scalar1=wk1[:, 0:1], scalar2=None,
                                                       op0=ALU.mult))
                P.vop(lambda e, tt=tt: e.scalar_tensor_tensor(
                    out=gsc[:, tt, :], in0=c2a[:, tt, :], scalar=wk2[:, 0:1], in1=t8[:],
                    op0=ALU.mult, op1=ALU.add))

            for ex in range(NE):
                hv = up_and_h(
                    lambda jt, ex=ex: w_upg[ex, :, jt * 128:(jt + 1) * 128]
                    .rearrange("(c p) w -> p c w", p=128),
                    lambda jt, ex=ex: w_upg[ex, :, IEXP + jt * 128:IEXP + (jt + 1) * 128]
                    .rearrange("(c p) w -> p c w", p=128),
                    11)
                down_proj(
                    lambda fg, ex=ex: w_dn[ex, :, fg * 512:(fg + 1) * 512]
                    .rearrange("(c p) w -> p c w", p=128),
                    11, lambda tt, ex=ex: gsc[:, tt, ex:ex + 1], None, hv)

            P.barrier()
            P.dma("sync", rdst, r_res[:, :, :])
        P.barrier()

    def phase_mamba(r_src_d, r_dst_d):
        with ExitStack() as pctx:
            r_res = sb(None, [128, NT, D], st=pctx)
            rsrc = r_src_d.rearrange("(t p) d -> p t d", p=128)
            rdst = r_dst_d.rearrange("(t p) d -> p t d", p=128)
            ldv = P.dma("sync", r_res[:, :, :], rsrc, sem="dR0")
            with ExitStack() as lctx:
                xh_tok = sb(None, [128, NT, D], BF16, st=lctx)
                emit_ln2(lambda tt: (r_res[:, tt, :], ("dR0", ldv)), xh_tok, lctx)
                P.barrier()

            x_fm = sb(None, [128, 32, TLOC], BF16, st=pctx)
            w_grp = sb(None, [128, 8, TLOC], BF16, st=pctx)
            z_tmp = sb(None, [128, TLOC], BF16, st=pctx)
            wb = WBuf(P, "wb_mam", 2, 32 * 128, st=pctx)
            pb = [ps_new(pctx) for _ in range(8)]
            rx = [0]

            for it in range(32):
                slab = wb.load(w_in_t[:, it * 128:(it + 1) * 128]
                               .rearrange("(c p) w -> p c w", p=128), DC, 128)
                wvw = wb.view(slab)
                lastS = 0
                for th in range(2):
                    ps = pb[rx[0] % 3]; rx[0] += 1
                    tv = mmg(ps, ps[:, :],
                             [(wvw[:, c, :], xh_fm[:, c, th * 512:(th + 1) * 512])
                              for c in range(DC)],
                             waits=[wb.ready[slab]])
                    sv = P.sop(lambda e, ps=ps, it=it, th=th: e.copy(
                        out=x_fm[:, it, th * 512:(th + 1) * 512], in_=ps[:, :]),
                        extra_waits=[("T", tv)])
                    consumed(ps, "S", sv)
                    lastS = sv
                wb.mark_free(slab, "S", lastS)

            wgrp_free_T = 0
            for g in range(4):
                for ig in range(8):
                    i = g * 8 + ig
                    zslab = wb.load(w_in_t[:, INTER + i * 128:INTER + (i + 1) * 128]
                                    .rearrange("(c p) w -> p c w", p=128), DC, 128)
                    cslab = wb.load(conv_t[:, i * 128:(i + 1) * 128]
                                    .rearrange("(c p) w -> p c w", p=128), 32, 128)
                    zv = wb.view(zslab); cvw = wb.view(cslab)
                    lastS = 0
                    for th in range(2):
                        ps = pb[rx[0] % 3]; rx[0] += 1
                        tv = mmg(ps, ps[:, :],
                                 [(zv[:, c, :], xh_fm[:, c, th * 512:(th + 1) * 512])
                                  for c in range(DC)],
                                 waits=[wb.ready[zslab]])
                        sv = P.sop(lambda e, ps=ps, th=th: e.activation(
                            out=z_tmp[:, th * 512:(th + 1) * 512], in_=ps[:, :],
                            func=AF.Silu),
                            extra_waits=[("T", tv), ("V", P.semv["V"])])
                        consumed(ps, "S", sv)
                        lastS = sv
                    lastV = 0
                    for th in range(2):
                        ps = pb[3 + (ig * 2 + th) % 3]
                        tv = mmg(ps, ps[:, :],
                                 [(cvw[:, c, :], x_fm[:, c, th * 512:(th + 1) * 512])
                                  for c in range(32)],
                                 waits=[wb.ready[cslab], ("S", lastS),
                                        ("T", wgrp_free_T) if (ig == 0 and th == 0)
                                        else ("T", 0)])
                        vv = P.vop(lambda e, ps=ps, ig=ig, th=th, i=i:
                                   e.scalar_tensor_tensor(
                            out=w_grp[:, ig, th * 512:(th + 1) * 512], in0=ps[:, :],
                            scalar=dD[:, i:i + 1],
                            in1=z_tmp[:, th * 512:(th + 1) * 512],
                            op0=ALU.mult, op1=ALU.mult),
                            extra_waits=[("T", tv)] +
                            ([("T", wgrp_free_T)] if (ig == 0 and th == 0) else []))
                        consumed(ps, "V", vv)
                        lastV = vv
                    wb.mark_free(zslab, "S", lastS)
                    wb.mark_free(cslab, "V", lastV)
                for fg in range(4):
                    slab = wb.load(
                        w_out_t[g * 1024:(g + 1) * 1024, fg * 512:(fg + 1) * 512]
                        .rearrange("(c p) w -> p c w", p=128), 8, 512)
                    wvw = wb.view(slab)
                    lastv = 0
                    for tt in range(NT):
                        ps = [pb[6], pb[7], pb[0], pb[1]][tt % 4]
                        tv = mmg(ps, ps[:, :],
                                 [(w_grp[:, c, tt * 128:(tt + 1) * 128], wvw[:, c, :])
                                  for c in range(8)],
                                 waits=[wb.ready[slab], ("V", lastV)])
                        vv = P.vop(lambda e, ps=ps, tt=tt, fg=fg:
                                   e.scalar_tensor_tensor(
                            out=r_res[:, tt, fg * 512:(fg + 1) * 512], in0=ps[:, :],
                            scalar=1.0, in1=r_res[:, tt, fg * 512:(fg + 1) * 512],
                            op0=ALU.mult, op1=ALU.add), extra_waits=[("T", tv)])
                        consumed(ps, "V", vv)
                        lastv = vv
                    wb.mark_free(slab, "V", lastv)
                wgrp_free_T = P.semv["T"]

            P.barrier()
            P.dma("sync", rdst, r_res[:, :, :])
        P.barrier()

    l1_state = {"free": [("T", 0), ("T", 0)], "pb": None}

    def emit_l1_og(og, wv_ap, ready_wait, r_res):
        pb = l1_state["pb"]
        lastT = 0
        for tt in range(NT):
            ps = pb[tt % 3]
            w = ([ready_wait] if tt == 0 else []) + [psfree[id(ps)]]
            pairs = [(xh_fm[:, c, tt * 128:(tt + 1) * 128], wv_ap[:, c, :])
                     for c in range(DC)]
            for i, (l, r) in enumerate(pairs):
                P.top(lambda e, l=l, r=r, st=(i == 0), ps=ps: e.matmul(
                    ps[:, :], l, r, start=st, stop=False),
                    waits=w if i == 0 else ())
            tv = P.top(lambda e, ps=ps, og=og: e.matmul(
                ps[:, :], onesrowf[:], l1b[0:1, og * 512:(og + 1) * 512],
                start=False, stop=True), incs=("T",))
            vv = P.vop(lambda e, ps=ps, tt=tt, og=og: e.scalar_tensor_tensor(
                out=r_res[:, tt, og * 512:(og + 1) * 512], in0=ps[:, :], scalar=0.5,
                in1=r_res[:, tt, og * 512:(og + 1) * 512], op0=ALU.mult, op1=ALU.add),
                extra_waits=[("T", tv)])
            consumed(ps, "V", vv)
            lastT = tv
        l1_state["free"][og % 2] = ("T", lastT)

    def phase_l1ffn(r_src_d, r_dst_d, call_idx):
        with ExitStack() as pctx:
            r_res = sb(None, [128, NT, D], st=pctx)
            rsrc = r_src_d.rearrange("(t p) d -> p t d", p=128)
            ldv = P.dma("sync", r_res[:, :, :], rsrc, sem="dR0")
            with ExitStack() as lctx:
                xh_tok = sb(None, [128, NT, D], BF16, st=lctx)
                emit_ln2(lambda tt: (r_res[:, tt, :], ("dR0", ldv)), xh_tok, lctx)
                P.barrier()

            weff = sb(None, [128, 2, DC, 512], BF16, st=pctx)
            pb = [ps_new(pctx) for _ in range(4)]
            l1_state["free"] = [("T", 0), ("T", 0)]
            l1_state["pb"] = pb

            if call_idx == 0:
                wf = sb(None, [128, 2, DC, 256], st=pctx)
                acc = sb(None, [128, 1], st=pctx)
                scal = sb(None, [128, 4], st=pctx)
                srow = sb(None, [1, 4], st=pctx)
                tmp1 = sb(None, [128, 1], st=pctx)
                pos_t = sb(None, [128, DC, 256], BF16, st=pctx)
                neg_t = sb(None, [128, DC, 256], BF16, st=pctx)
                wstore = {0: 0, 1: 0}
                P.vop(lambda e: e.memset(acc[:], 0.0))
                for i in range(8):
                    dv = P.dma("gpsimd", wf[:, i % 2, :, :],
                               l1w_t[:, i * 256:(i + 1) * 256]
                               .rearrange("(c p) w -> p c w", p=128),
                               waits=[("V", P.semv["V"])] if i >= 2 else (),
                               sem=f"dW{i % 2}")
                    P.vop(lambda e, i=i: e.tensor_reduce(
                        out=tmp1[:], in_=wf[:, i % 2, :, :], axis=AX.XY, op=ALU.add,
                        apply_absolute_value=True), extra_waits=[(f"dW{i % 2}", dv)])
                    P.vop(lambda e: e.tensor_tensor(out=acc[:], in0=acc[:],
                                                    in1=tmp1[:], op=ALU.add))
                vv = P.semv["V"]
                tv = mmg(pb[3], pb[3][0:1, 0:1], [(acc[:], onescolf[:])],
                         waits=[("V", vv)])
                vv = P.vop(lambda e: e.tensor_scalar(out=srow[0:1, 0:1],
                                                     in0=pb[3][0:1, 0:1],
                                                     scalar1=1.0 / (D * D),
                                                     scalar2=None, op0=ALU.mult),
                           extra_waits=[("T", tv)])
                consumed(pb[3], "V", vv)
                P.vop(lambda e: e.tensor_scalar(out=srow[0:1, 1:2], in0=srow[0:1, 0:1],
                                                scalar1=1e-10, scalar2=0.5,
                                                op0=ALU.add, op1=ALU.mult))
                vv = P.vop(lambda e: e.tensor_scalar(out=srow[0:1, 2:3],
                                                     in0=srow[0:1, 1:2],
                                                     scalar1=-1.0, scalar2=None,
                                                     op0=ALU.mult))
                tv = mmg(pb[3], pb[3][:, 0:3], [(onesrowf[:], srow[0:1, 0:3])],
                         waits=[("V", vv)])
                vv = P.vop(lambda e: e.tensor_copy(out=scal[:, 0:3], in_=pb[3][:, 0:3]),
                           extra_waits=[("T", tv)])
                consumed(pb[3], "V", vv)
                for og in range(4):
                    for half in range(2):
                        j0 = og * 512 + half * 256
                        dv = P.dma("gpsimd", wf[:, half, :, :],
                                   l1w_t[:, j0:j0 + 256]
                                   .rearrange("(c p) w -> p c w", p=128),
                                   waits=[("V", P.semv["V"])], sem=f"dW{half}")
                        P.vop(lambda e, half=half: e.tensor_scalar(
                            out=pos_t[:], in0=wf[:, half, :, :], scalar1=scal[:, 1:2],
                            scalar2=None, op0=ALU.is_gt),
                            extra_waits=[(f"dW{half}", dv)])
                        P.vop(lambda e, half=half: e.tensor_scalar(
                            out=neg_t[:], in0=wf[:, half, :, :], scalar1=scal[:, 2:3],
                            scalar2=None, op0=ALU.is_lt))
                        P.vop(lambda e: e.tensor_tensor(out=pos_t[:], in0=pos_t[:],
                                                        in1=neg_t[:], op=ALU.subtract))
                        vv = P.vop(lambda e, og=og, half=half: e.tensor_scalar(
                            out=weff[:, og % 2, :, half * 256:(half + 1) * 256],
                            in0=pos_t[:], scalar1=scal[:, 0:1], scalar2=None,
                            op0=ALU.mult),
                            extra_waits=([l1_state["free"][og % 2],
                                          (f"dV{og % 2}", wstore[og % 2])]
                                         if half == 0 else []))
                    wstore[og % 2] = P.dma(
                        "sync", weff_d[:, og * 512:(og + 1) * 512]
                        .rearrange("(c p) w -> p c w", p=128),
                        weff[:, og % 2, :, :], waits=[("V", vv)], sem=f"dV{og % 2}")
                    emit_l1_og(og, weff[:, og % 2, :, :], ("V", vv), r_res)
            else:
                for og in range(4):
                    dv = P.dma("gpsimd", weff[:, og % 2, :, :],
                               weff_d[:, og * 512:(og + 1) * 512]
                               .rearrange("(c p) w -> p c w", p=128),
                               waits=[l1_state["free"][og % 2]], sem=f"dW{og % 2}")
                    emit_l1_og(og, weff[:, og % 2, :, :], (f"dW{og % 2}", dv), r_res)

            P.barrier()
            P.dma("sync", r_dst_d.rearrange("(t p) d -> p t d", p=128),
                  r_res[:, :, :])
        P.barrier()

    def phase_attn(r_src_d, r_dst_d):
        with ExitStack() as pctx:
            rsrc = r_src_d.rearrange("(t p) d -> p t d", p=128)
            rdst = r_dst_d.rearrange("(t p) d -> p t d", p=128)
            with ExitStack() as lctx:
                r_io = sb(None, [128, 2, D], st=lctx)
                xh_tok = sb(None, [128, NT, D], BF16, st=lctx)
                lastv = [0] * NT

                def rtile(tt):
                    par = tt % 2
                    dv = P.dma("sync", r_io[:, par, :], rsrc[:, tt, :],
                               waits=[("V", lastv[tt - 2])] if tt >= 2 else (),
                               sem=f"dR{par}")
                    return r_io[:, par, :], (f"dR{par}", dv)
                emit_ln2(lambda tt: rtile(tt), xh_tok, lctx, ln_apply_vs=lastv)
                P.barrier()

            wb = WBuf(P, "wb_attn", 2, 16 * 512, st=pctx)
            pb = [ps_new(pctx) for _ in range(8)]
            kv_sb = sb(None, [128, 2, 512], BF16, st=pctx)
            kvw = {0: 0, 1: 0}
            for dc in range(DC):
                slab = wb.load(wk[:, dc * 128:(dc + 1) * 128]
                               .rearrange("(c p) w -> p c w", p=128), DC, 128)
                wvw = wb.view(slab)
                lastS = 0
                for th in range(2):
                    ps = pb[th]
                    tv = mmg(ps, ps[:, :],
                             [(wvw[:, c, :], xh_fm[:, c, th * 512:(th + 1) * 512])
                              for c in range(DC)],
                             waits=[wb.ready[slab]])
                    sv = P.sop(lambda e, ps=ps, th=th: e.copy(
                        out=kv_sb[:, th, :], in_=ps[:, :]),
                        extra_waits=[("T", tv), (f"dV{th}", kvw[th])])
                    consumed(ps, "S", sv)
                    kvw[th] = P.dma(
                        "sync",
                        kv_loc[dc, 0:128 * TLOC].rearrange("(p w) -> p w", w=TLOC)[:, th * 512:(th + 1) * 512],
                        kv_sb[:, th, :], waits=[("S", sv)], sem=f"dV{th}")
                    lastS = sv
                wb.mark_free(slab, "S", lastS)
            for og in range(4):
                slab = wb.load(wv[:, og * 512:(og + 1) * 512]
                               .rearrange("(c p) w -> p c w", p=128), DC, 512)
                wvw = wb.view(slab)
                lastS = 0
                for tt in range(NT):
                    par = tt % 2
                    ps = pb[2 + par]
                    tv = mmg(ps, ps[:, :],
                             [(xh_fm[:, c, tt * 128:(tt + 1) * 128], wvw[:, c, :])
                              for c in range(DC)],
                             waits=[wb.ready[slab]])
                    sv = P.sop(lambda e, ps=ps, par=par: e.copy(
                        out=kv_sb[:, par, :], in_=ps[:, :]),
                        extra_waits=[("T", tv), (f"dV{par}", kvw[par])])
                    consumed(ps, "S", sv)
                    dv = 0
                    for hh in range(4):
                        h = og * 4 + hh
                        base = 128 * TLOC + tt * 128 * 128
                        dv = P.dma("sync",
                                   kv_loc[h, base:base + 128 * 128]
                                   .rearrange("(p w) -> p w", w=128),
                                   kv_sb[:, par, hh * 128:(hh + 1) * 128],
                                   waits=[("S", sv)], sem=f"dV{par}")
                    kvw[par] = dv
                    lastS = sv
                wb.mark_free(slab, "S", lastS)

            P.wait("gpsimd", "dV0", P.semv["dV0"])
            P.wait("gpsimd", "dV1", P.semv["dV1"])
            cv = P.op("gpsimd", lambda e: e.collective_compute(
                "AllGather", ALU.bypass, replica_groups=[[0, 1, 2, 3], [4, 5, 6, 7]],
                ins=[kv_loc.opt()], outs=[kv_gath.opt()]),
                incs=("C",), waits=[("G", P.semv["G"])])

            attnout = sb(None, [128, DC, TLOC], BF16, st=pctx)
            midctx = ExitStack()
            k_head = sb(None, [128, 32, 128], BF16, st=midctx)
            v_head = sb(None, [128, 32, 128], BF16, st=midctx)
            q_head = sb(None, [128, 512], BF16, st=midctx)
            p_sb = sb(None, [128, 4, 512], BF16, st=midctx)
            ms_sb = sb(None, [128, 4, 512], st=midctx)
            rz = sb(None, [1, 512], st=midctx)
            rzbc = sb(None, [128, 512], st=midctx)
            mask_sb = sb(None, [128, 32, 512], mybir.dt.uint8, st=midctx)

            kv_free_T = 0
            q_free = ("S", 0)
            for qt in range(2):
                mq = P.dma("sync", mask_sb[:, :, :],
                           mask_in[qt].rearrange("s p w -> p s w"),
                           waits=[("V", P.semv["V"])], sem="dM")
                for h in range(HEADS):
                    kds = 0
                    for g in range(8):
                        a = g if g < 4 else 7 - g
                        half = 0 if g < 4 else 1
                        P.dma("sync", k_head[:, g * 4:(g + 1) * 4, :],
                              kv_gath[a, h, 0:128 * TLOC]
                              .rearrange("(p w) -> p w", w=TLOC)
                              [:, half * 512:(half + 1) * 512]
                              .rearrange("p (s w) -> p s w", w=128),
                              waits=[("C", cv), ("T", kv_free_T)], sem="dK")
                        base = 128 * TLOC + half * 512 * 128
                        kds = P.dma("sync", v_head[:, g * 4:(g + 1) * 4, :],
                                    kv_gath[a, h, base:base + 512 * 128]
                                    .rearrange("(s p w) -> p s w", p=128, w=128),
                                    waits=[("C", cv), ("T", kv_free_T)], sem="dK")
                    qslab = wb.load(wq[:, h * 128:(h + 1) * 128]
                                    .rearrange("(c p) w -> p c w", p=128), DC, 128)
                    qvw = wb.view(qslab)
                    ps = pb[6]
                    tv = mmg(ps, ps[:, :],
                             [(qvw[:, c, :], xh_fm[:, c, qt * 512:(qt + 1) * 512])
                              for c in range(DC)],
                             waits=[wb.ready[qslab], q_free])
                    sv = P.sop(lambda e, ps=ps: e.copy(out=q_head[:], in_=ps[:, :]),
                               extra_waits=[("T", tv)])
                    consumed(ps, "S", sv)
                    q_free = ("S", sv)
                    wb.mark_free(qslab, "S", sv)

                    psO = pb[4 + h % 2]
                    psZa = pb[7]
                    lastZT = 0
                    lastPV = 0
                    pv_done = {i: ("T", 0) for i in range(4)}
                    exp_done = {i: 0 for i in range(4)}
                    sc_t = [0] * 32

                    def emit_scores(st, first_waits=()):
                        par = st % 4
                        psS = pb[par]
                        sc_t[st] = mmg(psS, psS[:, :],
                                       [(k_head[:, st, :], q_head[:])],
                                       waits=list(first_waits))
                    emit_scores(0, [("dK", kds), ("S", sv), ("dM", mq)])
                    for st in range(1, 4):
                        emit_scores(st)
                    for st in range(32):
                        par = st % 4
                        psS = pb[par]
                        vv = P.vop(lambda e, st=st, par=par, psS=psS: e.tensor_tensor(
                            out=ms_sb[:, par, :], in0=psS[:, :],
                            in1=mask_sb[:, st, :], op=ALU.mult),
                            extra_waits=[("T", sc_t[st]), ("S", exp_done[par])])
                        consumed(psS, "V", vv)
                        se = P.sop(lambda e, par=par: e.activation(
                            out=p_sb[:, par, :], in_=ms_sb[:, par, :], func=AF.Exp,
                            scale=1.0 / 64.0),
                            extra_waits=[("V", vv), pv_done[par]])
                        exp_done[par] = se
                        tz = P.top(lambda e, st=st: e.matmul(
                            psZa[0:1, :], onescol[:], p_sb[:, st % 4, :],
                            start=(st == 0), stop=(st == 31)),
                            incs=("T",) if st == 31 else (),
                            waits=[("S", se)] +
                                  ([psfree[id(psZa)]] if st == 0 else []))
                        tpv = P.top(lambda e, psO=psO, st=st: e.matmul(
                            psO[:, :], v_head[:, st, :], p_sb[:, st % 4, :],
                            start=(st == 0), stop=(st == 31)),
                            incs=("T",),
                            waits=[psfree[id(psO)]] if st == 0 else ())
                        pv_done[par] = ("T", tpv)
                        if st + 4 < 32:
                            emit_scores(st + 4)
                        if st == 31:
                            lastZT = tz
                            lastPV = tpv
                    vv = P.vop(lambda e: e.reciprocal(out=rz[0:1, :], in_=psZa[0:1, :]),
                               extra_waits=[("T", lastZT)])
                    consumed(psZa, "V", vv)
                    tb = mmg(pb[6], pb[6][:, :], [(onesrowf[:], rz[0:1, :])],
                             waits=[("V", vv)])
                    vv = P.vop(lambda e: e.tensor_copy(out=rzbc[:], in_=pb[6][:, :]),
                               extra_waits=[("T", tb)])
                    consumed(pb[6], "V", vv)
                    vv = P.vop(lambda e, h=h, qt=qt, psO=psO: e.tensor_tensor(
                        out=attnout[:, h, qt * 512:(qt + 1) * 512], in0=psO[:, :],
                        in1=rzbc[:], op=ALU.mult), extra_waits=[("T", lastPV)])
                    consumed(psO, "V", vv)
                    kv_free_T = P.semv["T"]

            P.barrier()
            midctx.close()
            attn_y = sb(None, [128, NT, D], BF16, st=pctx)
            r_io = sb(None, [128, 2, D], st=pctx)
            for og in range(4):
                slab = wb.load(wo[:, og * 512:(og + 1) * 512]
                               .rearrange("(c p) w -> p c w", p=128), DC, 512)
                wvw = wb.view(slab)
                lastv = 0
                for tt in range(NT):
                    ps = pb[tt % 3]
                    tv = mmg(ps, ps[:, :],
                             [(attnout[:, c, tt * 128:(tt + 1) * 128], wvw[:, c, :])
                              for c in range(DC)],
                             waits=[wb.ready[slab]])
                    vv = P.vop(lambda e, ps=ps, tt=tt, og=og: e.tensor_copy(
                        out=attn_y[:, tt, og * 512:(og + 1) * 512], in_=ps[:, :]),
                        extra_waits=[("T", tv)])
                    consumed(ps, "V", vv)
                    lastv = vv
                wb.mark_free(slab, "V", lastv)
            stored = {0: 0, 1: 0}
            for tt in range(NT):
                par = tt % 2
                dv = P.dma("sync", r_io[:, par, :], rsrc[:, tt, :],
                           waits=[(f"dV{par}", stored[par]),
                                  ("V", P.semv["V"] if tt < 2 else 0)],
                           sem=f"dR{par}")
                vv = P.vop(lambda e, tt=tt, par=par: e.tensor_tensor(
                    out=r_io[:, par, :], in0=r_io[:, par, :], in1=attn_y[:, tt, :],
                    op=ALU.add), extra_waits=[(f"dR{par}", dv)])
                stored[par] = P.dma("sync", rdst[:, tt, :], r_io[:, par, :],
                                    waits=[("V", vv)], sem=f"dV{par}")
        P.barrier()

    phases = [
        lambda s, d: phase_moe(s, d, 0),
        lambda s, d: phase_mamba(s, d),
        lambda s, d: phase_moe(s, d, 1),
        lambda s, d: phase_l1ffn(s, d, 0),
        lambda s, d: phase_attn(s, d),
        lambda s, d: phase_l1ffn(s, d, 1),
    ][:n_phases]
    src_ap = x0
    for i, ph in enumerate(phases):
        dst = out_d if i == len(phases) - 1 else r_d[i]
        ph(src_ap, dst)
        src_ap = dst

    P.wait("sync", "dS", P.semv["dS"])
    P.finalize()
    ctx.close()
    return nc


_PROG_CACHE = {}


def _get_prog():
    key = ("nc", _N_PHASES)
    if key not in _PROG_CACHE:
        _PROG_CACHE[key] = build_program(_N_PHASES)
    return _PROG_CACHE[key]


def _consts():
    ident = np.eye(128, dtype=BF16NP)
    u128 = (np.arange(128)[:, None] < np.arange(128)[None, :]).astype(BF16NP)
    return dict(
        ident_in=ident, u128_in=u128,
        onescol_in=np.ones((128, 1), dtype=BF16NP),
        onescolf_in=np.ones((128, 1), dtype=np.float32),
        onesrowf_in=np.ones((1, 128), dtype=np.float32),
        iota8_in=np.broadcast_to(np.arange(8, dtype=np.float32), (128, 8)).copy(),
        iota9_in=np.broadcast_to(np.arange(9, 17, dtype=np.float32), (128, 8)).copy(),
        big8_in=np.full((128, 8), 8.0, dtype=np.float32),
        neg8_in=np.full((128, 8), -1e30, dtype=np.float32),
        onestile_in=np.ones((128, 512), dtype=BF16NP),
        epscol_in=np.full((128, 1), 1e-5, dtype=np.float32),
    )


def kernel(**inputs):
    x = np.asarray(inputs["L0_attn_hidden_states"], dtype=np.float32)
    shared = dict(
        w_in_t=np.ascontiguousarray(np.asarray(inputs["L0_attn_W_in"], np.float32).T),
        conv_t=np.ascontiguousarray(np.asarray(inputs["L0_attn_conv_w"], np.float32).T),
        w_out_t=np.ascontiguousarray(np.asarray(inputs["L0_attn_W_out"], np.float32).T),
        dvec=np.ascontiguousarray(
            np.asarray(inputs["L0_attn_D"], np.float32).reshape(32, 128).T),
        router=np.ascontiguousarray(np.asarray(inputs["L0_ffn_router"], np.float32)),
        w_upg=np.ascontiguousarray(np.asarray(inputs["L0_ffn_w_up_gate"], np.float32)),
        w_dn=np.ascontiguousarray(np.asarray(inputs["L0_ffn_w_down"], np.float32)),
        sh_up=np.ascontiguousarray(
            np.asarray(inputs["L0_ffn_shared_w_up_gate"], np.float32)),
        sh_dn=np.ascontiguousarray(
            np.asarray(inputs["L0_ffn_shared_w_down"], np.float32)),
        wq=np.ascontiguousarray(np.asarray(inputs["L1_attn_Wq"], np.float32)),
        wk=np.ascontiguousarray(np.asarray(inputs["L1_attn_Wk"], np.float32)),
        wv=np.ascontiguousarray(np.asarray(inputs["L1_attn_Wv"], np.float32)),
        wo=np.ascontiguousarray(np.asarray(inputs["L1_attn_Wo"], np.float32)),
        l1w_t=np.ascontiguousarray(np.asarray(inputs["L1_ffn_weight"], np.float32).T),
        l1b_row=np.ascontiguousarray(
            np.asarray(inputs["L1_ffn_bias"], np.float32).reshape(1, 2048)),
        **_consts(),
    )

    stripes = [(c // 4, c % 4, 7 - c % 4) for c in range(8)]

    def gstripe(rank, half):
        b, j = rank // 4, rank % 4
        return b * 8 + (j if half == 0 else 7 - j)

    in_maps = []
    for c in range(8):
        b, gA, gB = stripes[c]
        x0 = np.concatenate([x[b, gA * 512:(gA + 1) * 512],
                             x[b, gB * 512:(gB + 1) * 512]], axis=0)
        gAg, gBg = b * 8 + gA, b * 8 + gB
        offa = np.zeros((16, 1), np.float32)
        offb = np.zeros((16, 1), np.float32)
        for r2 in range(16):
            g = gstripe(r2 // 2, r2 % 2)
            if g < gAg:
                offa[r2, 0] = 1.0
            if g < gBg and r2 != c * 2:
                offb[r2, 0] = 1.0
        qoffs = [gA * 512, gB * 512]
        srange = np.arange(SEQ).reshape(32, 128)
        mask = np.zeros((2, 32, 128, 512), dtype=np.uint8)
        for qt in range(2):
            qglob = qoffs[qt] + np.arange(512)
            mask[qt] = (srange[:, :, None] <= qglob[None, None, :]).astype(np.uint8)
        in_maps.append(dict(x0=np.ascontiguousarray(x0), mask_in=mask,
                            offa_in=offa, offb_in=offb, **shared))

    nc = _get_prog()
    res = run_bass_kernel_spmd(nc, in_maps, core_ids=list(range(8)))

    out = np.zeros((2, 4096, 2048), dtype=np.float32)
    for c in range(8):
        b, gA, gB = stripes[c]
        o = np.asarray(res.results[c]["out"])
        out[b, gA * 512:(gA + 1) * 512] = o[:512]
        out[b, gB * 512:(gB + 1) * 512] = o[512:]
    return out
